# version 12
# baseline (speedup 1.0000x reference)
import contextlib
import math

import numpy as np
import concourse.bass as bass
import concourse.mybir as mybir
from concourse.bass_utils import run_bass_kernel_spmd

F32 = mybir.dt.float32
BF16 = mybir.dt.bfloat16
AF = mybir.ActivationFunctionType
ALU = mybir.AluOpType

NCORES = 8
D = 2048
DC = 16
HID = 8192
HC = 64
NH = 16
HD = 128
CH = 512
NCH_OWN = 4
T_OWN = CH * NCH_OWN
T_CORE = T_OWN + CH
SCALE = 1.0 / math.sqrt(HD)
EXP_BIAS = -30.0
G1_SCALE = 0.5 / math.sqrt(HID)
LN_EPS = 1e-5

ST_LO = [max(0, 128 * t - 512) for t in range(8)]
ST_HI = [min(512, 128 * t + 128) for t in range(8)]
ST_W = [h - l for l, h in zip(ST_LO, ST_HI)]
ST_OFF = [0]
for _t in range(8):
    ST_OFF.append(ST_OFF[-1] + ST_W[_t])
MASK_COLS = ST_OFF[-1]


def _hadamard(n):
    h = np.array([[1.0]], dtype=np.float32)
    while h.shape[0] < n:
        h = np.block([[h, h], [h, -h]])
    return np.ascontiguousarray(h.astype(np.float32))


def _band_mask(first_restrict):
    m = np.zeros((128, MASK_COLS), dtype=np.float32)
    for t in range(8):
        p = np.arange(128)[:, None]
        j = np.arange(ST_W[t])[None, :]
        tk = 128 * t + p
        tq = ST_LO[t] + j
        v = ((tk >= tq) & (tk <= tq + 512)).astype(np.float32)
        if first_restrict and t < 4:
            v[:] = 0.0
        m[:, ST_OFF[t]:ST_OFF[t] + ST_W[t]] = v
    return m


class Buf:
    __slots__ = ("name", "wl", "rl", "pend_r")

    def __init__(self, name):
        self.name = name
        self.wl = {}
        self.rl = {}
        self.pend_r = set()


PHYS = ("pe", "act", "dve", "pool", "sp")
COMP_SEM = {"pe": "pe", "act": "act", "dve": "dve", "pool": "pool", "sp": "sp_c"}
DMA_ROT = {"act": ("qa", 8), "pool": ("qg", 8), "sp": ("qs", 12)}
SEM_NAMES = tuple(["pe", "act", "dve", "pool", "sp_c"]
                  + [f"qa{i}" for i in range(8)]
                  + [f"qg{i}" for i in range(8)]
                  + [f"qs{i}" for i in range(12)])


class Prog:
    def __init__(self):
        self.ops = {p: [] for p in PHYS}
        self.cnt = {s: 0 for s in SEM_NAMES}
        self.waited = {p: {} for p in PHYS}
        self.pend_bufs = {p: [] for p in PHYS}
        self.qrot = {p: 0 for p in PHYS}

    def _resolve(self, phys, evt):
        for pb in self.pend_bufs[phys]:
            if phys in pb.pend_r:
                pb.pend_r.discard(phys)
                k, v = evt
                if pb.rl.get(k, 0) < v:
                    pb.rl[k] = v
        self.pend_bufs[phys] = []

    def _force_inc(self, phys):
        key = COMP_SEM[phys]
        ops = self.ops[phys]
        for i in range(len(ops) - 1, -1, -1):
            waits, fn, ik, amt = ops[i]
            if fn is None:
                continue
            if ik is None:
                self.cnt[key] += 1
                ops[i] = (waits, fn, key, 1)
                self._resolve(phys, (key, self.cnt[key]))
                return
            break
        self.cnt[key] += 1
        ops.append(({}, None, key, 1))
        self._resolve(phys, (key, self.cnt[key]))

    def emit(self, phys, fn, reads=(), writes=(), first_writes=(),
             writes_merge=(), inc=False, dma=False):
        waits = {}
        W = self.waited[phys]

        def need(k, v):
            if phys == "pe" and k == "pe":
                return
            if W.get(k, 0) >= v:
                return
            if waits.get(k, 0) < v:
                waits[k] = v

        for b in reads:
            for k, v in b.wl.items():
                need(k, v)
        for b in tuple(writes) + tuple(first_writes):
            for k, v in b.wl.items():
                need(k, v)
            for p in tuple(b.pend_r):
                self._force_inc(p)
            for k, v in b.rl.items():
                need(k, v)
        for b in writes_merge:
            for p in tuple(b.pend_r):
                self._force_inc(p)
            for k, v in b.rl.items():
                need(k, v)

        if dma:
            pref, R = DMA_ROT[phys]
            semkey = f"{pref}{self.qrot[phys] % R}"
            self.qrot[phys] += 1
            amt = 16
            if self.cnt[semkey] > 0:
                need(semkey, self.cnt[semkey])
        else:
            semkey = COMP_SEM[phys]
            amt = 1
        if inc:
            self.cnt[semkey] += amt
            evt = (semkey, self.cnt[semkey])
        else:
            evt = None

        self.ops[phys].append((waits, fn, semkey if inc else None, amt))
        for k, v in waits.items():
            if W.get(k, 0) < v:
                W[k] = v

        for b in reads:
            if evt is not None:
                if b.rl.get(evt[0], 0) < evt[1]:
                    b.rl[evt[0]] = evt[1]
                b.pend_r.discard(phys)
            else:
                if phys not in b.pend_r:
                    b.pend_r.add(phys)
                    self.pend_bufs[phys].append(b)
        if inc and not dma:
            self._resolve(phys, evt)
        for b in writes:
            assert inc, f"write to {b.name} without inc"
            b.wl = {evt[0]: evt[1]}
            b.rl = {}
            b.pend_r = set()
        for b in writes_merge:
            assert inc, f"merge-write to {b.name} without inc"
            if b.wl.get(evt[0], 0) < evt[1]:
                b.wl[evt[0]] = evt[1]


def build_nc():
    nc = bass.Bass("TRN2", target_bir_lowering=False, debug=False)

    xin = nc.dram_tensor("xin", [D, T_CORE], F32, kind="ExternalInput").ap()
    w1t = nc.dram_tensor("w1t", [D, HID], F32, kind="ExternalInput").ap()
    w3t = nc.dram_tensor("w3t", [D, HID], F32, kind="ExternalInput").ap()
    w2t = nc.dram_tensor("w2t", [HID, D], F32, kind="ExternalInput").ap()
    wqkvt = nc.dram_tensor("wqkvt", [D, 3 * D], F32, kind="ExternalInput").ap()
    woutt = nc.dram_tensor("woutt", [D, D], F32, kind="ExternalInput").ap()
    h64_in = nc.dram_tensor("h64", [64, 64], F32, kind="ExternalInput").ap()
    h128_in = nc.dram_tensor("h128", [128, 128], F32, kind="ExternalInput").ap()
    mfirst = nc.dram_tensor("mfirst", [128, MASK_COLS], F32, kind="ExternalInput").ap()
    mrest = nc.dram_tensor("mrest", [128, MASK_COLS], F32, kind="ExternalInput").ap()
    out = nc.dram_tensor("out", [D, T_OWN], F32, kind="ExternalOutput").ap()

    s_w1 = nc.dram_tensor("s_w1", [HC, 128, 2048], BF16).ap()
    s_w3 = nc.dram_tensor("s_w3", [HC, 128, 2048], BF16).ap()
    s_g1t = nc.dram_tensor("s_g1t", [HC, DC, 128, 128], BF16).ap()
    s_g1p = nc.dram_tensor("s_g1p", [8, DC, 128, 1024], BF16).ap()
    s_qk = nc.dram_tensor("s_qk", [32, 128, 2048], BF16).ap()
    s_wv = nc.dram_tensor("s_wv", [4, DC, 128, 512], BF16).ap()
    s_wout = nc.dram_tensor("s_wout", [DC, 128, 2048], BF16).ap()

    P = Prog()
    emit = P.emit

    with contextlib.ExitStack() as ctx:
        ent = ctx.enter_context

        xbuf = ent(nc.sbuf_tensor("xbuf", [128, DC * CH], F32))
        xln = ent(nc.sbuf_tensor("xln", [128, DC * CH], BF16))
        arena = ent(nc.sbuf_tensor("arena", [128, 11264], BF16))
        z1r = ent(nc.sbuf_tensor("z1r", [128, 8 * CH], BF16))
        silu_sb = ent(nc.sbuf_tensor("silu_sb", [128, 2 * CH], F32))
        hbf = ent(nc.sbuf_tensor("hbf", [128, 2 * CH], BF16))
        kc = [ent(nc.sbuf_tensor("kc0", [128, NH * CH], BF16)),
              ent(nc.sbuf_tensor("kc1", [128, NH * CH], BF16))]
        vc = [ent(nc.sbuf_tensor("vc0", [128, 4 * D], BF16)),
              ent(nc.sbuf_tensor("vc1", [128, 4 * D], BF16))]
        w1b = ent(nc.sbuf_tensor("w1b", [128, 3 * 1024], BF16))
        w3b = ent(nc.sbuf_tensor("w3b", [128, 3 * 1024], BF16))
        g1b = ent(nc.sbuf_tensor("g1b", [128, 6 * 1024], BF16))
        qb = ent(nc.sbuf_tensor("qb", [128, 2 * 1024], BF16))
        kb = ent(nc.sbuf_tensor("kb", [128, 2 * 1024], BF16))
        wvb = ent(nc.sbuf_tensor("wvb", [128, 3 * 512], BF16))
        woutb = ent(nc.sbuf_tensor("woutb", [128, 3 * 1024], BF16))
        maskb = ent(nc.sbuf_tensor("maskb", [128, MASK_COLS], BF16))
        xsqb = ent(nc.sbuf_tensor("xsqb", [128, 2 * CH], BF16))
        xbfb = ent(nc.sbuf_tensor("xbfb", [128, 2 * CH], BF16))
        rbsb = ent(nc.sbuf_tensor("rbsb", [128, 2 * CH], F32))
        stats = ent(nc.sbuf_tensor("stats", [128, 2 * CH], F32))
        h64f = ent(nc.sbuf_tensor("h64f", [128, 64], F32))
        h128b = ent(nc.sbuf_tensor("h128b", [128, 128], BF16))
        onesM = ent(nc.sbuf_tensor("onesM", [128, 128], BF16))
        ebias = ent(nc.sbuf_tensor("ebias", [128, 1], F32))

        pb4 = [ent(nc.psum_tensor(f"pb{i}", [128, 512], F32)) for i in range(4)]
        pq = ent(nc.psum_tensor("pq", [128, 2048], F32))
        pb = pb4 + [pq[:, 512 * i:512 * i + 512] for i in range(4)]

        sems = {n: ent(nc.semaphore(f"s_{n}")) for n in SEM_NAMES}
        block = ent(nc.Block())

        B_x = [Buf(f"x{i}") for i in range(DC)]
        B_xln = [Buf(f"xln{i}") for i in range(DC)]
        B_obf = Buf("obf")
        B_qbf = [Buf("qbf0"), Buf("qbf1")]
        B_exp = [Buf("exp0"), Buf("exp1")]
        B_pbf = [Buf("pbf0"), Buf("pbf1")]
        B_z1 = [Buf(f"z1_{i}") for i in range(8)]
        B_silu = [Buf(f"silu{i}") for i in range(2)]
        B_hbf = [Buf(f"hbf{i}") for i in range(2)]
        B_kc = [Buf("kc0"), Buf("kc1")]
        B_vc = [Buf("vc0"), Buf("vc1")]
        B_w1 = [Buf(f"w1b{i}") for i in range(3)]
        B_w3 = [Buf(f"w3b{i}") for i in range(3)]
        B_g1 = [Buf(f"g1b{i}") for i in range(6)]
        B_qb = [Buf(f"qb{i}") for i in range(2)]
        B_kb = [Buf(f"kb{i}") for i in range(2)]
        B_wv = [Buf(f"wvb{i}") for i in range(3)]
        B_wout = [Buf(f"woutb{i}") for i in range(3)]
        B_mask = Buf("maskb")
        B_xsq = [Buf(f"xsq{i}") for i in range(2)]
        B_xbf = [Buf(f"xbf{i}") for i in range(2)]
        B_rb = [Buf(f"rb{i}") for i in range(2)]
        B_stats = [Buf(f"st{i}") for i in range(3)]
        B_pb = [Buf(f"pb{i}") for i in range(8)]
        B_h128 = Buf("h128")
        B_onesM = Buf("onesM")
        B_ebias = Buf("ebias")
        B_sw1 = [Buf(f"sw1_{i}") for i in range(16)]
        B_sw3 = [Buf(f"sw3_{i}") for i in range(16)]
        B_sg1 = Buf("sg1")
        B_sg1p = [Buf(f"sg1p_{i}") for i in range(8)]
        B_w2cf = [Buf("w2cf0"), Buf("w2cf1")]
        B_h64f = Buf("h64f")
        B_sqk = [Buf(f"sqk_{i}") for i in range(8)]
        B_swv = Buf("swv")
        B_swout = [Buf(f"swout_{i}") for i in range(4)]
        B_g1st = [Buf(f"g1s{i}") for i in range(4)]

        emit("pool", lambda g: g.dma_start(out=h64f[64:128, :], in_=h64_in),
             writes=[B_h64f], inc=True, dma=True)
        emit("pool", lambda g: g.dma_start(out=h128b[:, :], in_=h128_in),
             writes=[B_h128], inc=True, dma=True)
        emit("pool", lambda g: g.dma_start(out=maskb[:, :], in_=mfirst),
             writes=[B_mask], inc=True, dma=True)
        emit("dve", lambda v: v.memset(onesM[:, :], 1.0), writes=[B_onesM], inc=True)
        emit("dve", lambda v: v.memset(ebias[:, :], EXP_BIAS), writes=[B_ebias], inc=True)

        w1v = w1t.rearrange("(c p) (m f) -> m p c f", p=128, f=128)
        w3v = w3t.rearrange("(c p) (m f) -> m p c f", p=128, f=128)
        qkv_v = wqkvt[:, 0:4096].rearrange("(c p) (j f) -> j p c f", p=128, f=128)
        wv_v = wqkvt[:, 4096:6144].rearrange("(c p) (g f) -> g c p f", p=128, f=512)
        wout_v = woutt.rearrange("(c p) (j f) -> j p c f", p=128, f=128)

        def sub_cast(dst3, src3):
            return lambda g, dst3=dst3, src3=src3: g.dma_start(out=dst3, in_=src3)

        def w_sub(sten, view, m):
            return sub_cast(sten[m].rearrange("p (c f) -> p c f", f=128), view[m])

        w1_jobs = [[B_sw1[i]] + [w_sub(s_w1, w1v, 4 * i + j) for j in range(4)]
                   for i in range(16)]
        w3_jobs = [[B_sw3[i]] + [w_sub(s_w3, w3v, 4 * i + j) for j in range(4)]
                   for i in range(16)]
        tail_jobs = []
        for i in range(8):
            tail_jobs.append([B_sqk[i]] + [w_sub(s_qk, qkv_v, 4 * i + j) for j in range(4)])
        tail_jobs.append([B_swv] + [
            (lambda g, j=j: g.dma_start(out=s_wv[j:j + 1], in_=wv_v[j:j + 1]))
            for j in range(4)])
        for i in range(4):
            tail_jobs.append([B_swout[i]] + [w_sub(s_wout, wout_v, 4 * i + j)
                                             for j in range(4)])

        def emit_cast_job(job):
            b = job[0]
            for fn in job[1:]:
                emit("pool", fn, writes_merge=[b], inc=True, dma=True)

        for job in (w1_jobs[0], w3_jobs[0], w1_jobs[1], w3_jobs[1]):
            emit_cast_job(job)

        w2v = w2t.rearrange("(a b) o -> b a o", b=128)
        w2cf = [arena[64:128, 0:4096].bitcast(F32),
                arena[64:128, 4096:8192].bitcast(F32)]
        g1st = [arena[0:64, 0:2048], arena[0:64, 2048:4096],
                arena[0:64, 4096:6144], arena[0:64, 6144:8192]]

        def g1_store(bidx):
            p4 = bidx % 4
            g1st_v = g1st[p4].rearrange("p (o f) -> p o f", f=128)
            emit("act",
                 lambda s, g1st_v=g1st_v, bidx=bidx: s.dma_start(
                     out=s_g1t[:, :, bidx, :], in_=g1st_v),
                 reads=[B_g1st[p4]], writes=[B_sg1], inc=True, dma=True)

        for bidx in range(128):
            par = bidx % 2
            p4 = bidx % 4
            emit("sp",
                 lambda s, bidx=bidx, par=par: s.dma_start(out=w2cf[par], in_=w2v[bidx]),
                 writes=[B_w2cf[par]], inc=True, dma=True)
            boff = 4 * (bidx % 2)
            for j in range(4):
                jp = boff + j
                emit("pe",
                     lambda t, par=par, j=j, jp=jp: t.matmul(
                         pb[jp][0:64, :], h64f[64:128, :],
                         w2cf[par][:, 512 * j:512 * j + 512],
                         start=True, stop=True),
                     reads=[B_w2cf[par], B_h64f], first_writes=[B_pb[jp]],
                     writes=[B_pb[jp]], inc=True)
            for j in range(4):
                jb = boff + j
                eng = "dve" if (bidx + j) % 2 == 0 else "act"
                if eng == "act":
                    emit("act",
                         lambda s, p4=p4, j=j, jb=jb: s.activation(
                             g1st[p4][:, 512 * j:512 * j + 512],
                             pb[jb][0:64, :], AF.Copy, scale=G1_SCALE),
                         reads=[B_pb[jb]],
                         writes_merge=[B_g1st[p4]], inc=True)
                else:
                    emit("dve",
                         lambda v, p4=p4, j=j, jb=jb: v.tensor_scalar(
                             g1st[p4][:, 512 * j:512 * j + 512],
                             pb[jb][0:64, :], G1_SCALE, None, ALU.mult),
                         reads=[B_pb[jb]],
                         writes_merge=[B_g1st[p4]], inc=True)
            if bidx >= 2:
                g1_store(bidx - 2)
        g1_store(126)
        g1_store(127)

        for i in range(2, 16):
            emit_cast_job(w1_jobs[i])
            emit_cast_job(w3_jobs[i])

        for mb in range(8):
            for o in range(DC):
                rsrc = s_g1t[8 * mb:8 * mb + 8, o].rearrange("m p f -> p m f")
                rdst = s_g1p[mb, o].rearrange("p (m f) -> p m f", f=128)
                q = "act" if (mb * DC + o) % 2 else "sp"
                emit(q, lambda s, rsrc=rsrc, rdst=rdst: s.dma_start(out=rdst, in_=rsrc),
                     reads=[B_sg1], writes_merge=[B_sg1p[mb]], inc=True, dma=True)

        for job in tail_jobs:
            emit_cast_job(job)

        emit("dve", lambda v: v.memset(arena[0:1, 0:1], 0.0),
             writes=[B_w2cf[0], B_w2cf[1], B_g1st[0], B_g1st[1],
                     B_g1st[2], B_g1st[3],
                     B_obf, B_qbf[0], B_qbf[1], B_exp[0], B_exp[1],
                     B_pbf[0], B_pbf[1]],
             inc=True)

        def mm_group(bank, items):
            n = len(items)
            for i, (reg, lh, rh, reads) in enumerate(items):
                first = i == 0
                last = i == n - 1
                emit("pe",
                     lambda t, reg=reg, lh=lh, rh=rh, first=first, last=last:
                         t.matmul(reg, lh, rh, start=first, stop=last),
                     reads=list(reads),
                     first_writes=[B_pb[bank]] if first else (),
                     writes=[B_pb[bank]] if last else (),
                     inc=last)

        def ln_tile(c):
            par = c % 2
            emit("act",
                 lambda s, c=c, par=par: s.activation(
                     xsqb[:, par * CH:(par + 1) * CH],
                     xbuf[:, c * CH:(c + 1) * CH], AF.Square),
                 reads=[B_x[c]], writes=[B_xsq[par]], inc=True)
            emit("dve",
                 lambda v, c=c, par=par: v.tensor_copy(
                     xbfb[:, par * CH:(par + 1) * CH],
                     xbuf[:, c * CH:(c + 1) * CH]),
                 reads=[B_x[c]], writes=[B_xbf[par]], inc=True)
            emit("pe",
                 lambda t, c=c, par=par: t.matmul(
                     pb[0][:, :], onesM[:, :],
                     xbfb[:, par * CH:(par + 1) * CH],
                     start=(c == 0), stop=(c == DC - 1)),
                 reads=[B_xbf[par], B_onesM],
                 first_writes=[B_pb[0]] if c == 0 else (),
                 writes=[B_pb[0]] if c == DC - 1 else (),
                 inc=True)
            emit("pe",
                 lambda t, c=c, par=par: t.matmul(
                     pb[1][:, :], onesM[:, :],
                     xsqb[:, par * CH:(par + 1) * CH],
                     start=(c == 0), stop=(c == DC - 1)),
                 reads=[B_xsq[par], B_onesM],
                 first_writes=[B_pb[1]] if c == 0 else (),
                 writes=[B_pb[1]] if c == DC - 1 else (),
                 inc=True)

        def ln_finish():
            sA = stats[:, 0:CH]
            sB = stats[:, CH:2 * CH]
            sC = rbsb[:, 0:CH]
            B_stats[2] = B_rb[0]
            emit("dve", lambda v: v.tensor_scalar(sA, pb[0][:, :], -1.0 / D, None, ALU.mult),
                 reads=[B_pb[0]], writes=[B_stats[0]], inc=True)
            emit("dve", lambda v: v.tensor_scalar(sB, pb[1][:, :], 1.0 / D, LN_EPS, ALU.mult, ALU.add),
                 reads=[B_pb[1]], writes=[B_stats[1]], inc=True)
            emit("dve", lambda v: v.tensor_mul(sC, sA, sA),
                 reads=[B_stats[0]], writes=[B_stats[2]], inc=True)
            emit("dve", lambda v: v.tensor_sub(sB, sB, sC),
                 reads=[B_stats[1], B_stats[2]], writes=[B_stats[1]], inc=True)
            emit("act", lambda s: s.activation(sC, sB, AF.Ln),
                 reads=[B_stats[1]], writes=[B_stats[2]], inc=True)
            emit("act", lambda s: s.activation(sB, sC, AF.Exp, scale=-0.5),
                 reads=[B_stats[2]], writes=[B_stats[1]], inc=True)
            for c in range(DC):
                par = c % 2
                eng = "dve"
                if eng == "dve":
                    emit("dve",
                         lambda v, c=c, par=par: v.tensor_add(
                             silu_sb[:, par * CH:(par + 1) * CH],
                             xbuf[:, c * CH:(c + 1) * CH], sA),
                         reads=[B_x[c], B_stats[0]], writes=[B_silu[par]], inc=True)
                    emit("dve",
                         lambda v, c=c, par=par: v.tensor_mul(
                             xln[:, c * CH:(c + 1) * CH],
                             silu_sb[:, par * CH:(par + 1) * CH], sB),
                         reads=[B_silu[par], B_stats[1]], writes=[B_xln[c]], inc=True)
                else:
                    emit("pool",
                         lambda g, c=c, par=par: g.tensor_add(
                             silu_sb[:, par * CH:(par + 1) * CH],
                             xbuf[:, c * CH:(c + 1) * CH], sA),
                         reads=[B_x[c], B_stats[0]], writes=[B_silu[par]], inc=True)
                    emit("pool",
                         lambda g, c=c, par=par: g.tensor_mul(
                             xln[:, c * CH:(c + 1) * CH],
                             silu_sb[:, par * CH:(par + 1) * CH], sB),
                         reads=[B_silu[par], B_stats[1]], writes=[B_xln[c]], inc=True)

        ring_state = {}

        def stream(name, rings, bufs, fn_maker, dep=None):
            idx = ring_state.get(name, 0)
            ring_state[name] = idx + 1
            slot = idx % rings
            b = bufs[slot]
            emit("sp", fn_maker(slot), reads=[dep] if dep is not None else [],
                 writes=[b], inc=True, dma=True)
            return slot, b

        def ffn(tile_cb=None):
            def emit_h128(hp, mi):
                emit("pe",
                     lambda t, hp=hp: t.matmul(
                         pb[2][:, :], h128b[:, :], hbf[:, hp * CH:(hp + 1) * CH],
                         start=True, stop=True),
                     reads=[B_hbf[hp], B_h128], first_writes=[B_pb[2]],
                     writes=[B_pb[2]], inc=True)
                emit("act",
                     lambda s, mi=mi: s.activation(
                         z1r[:, mi * CH:(mi + 1) * CH], pb[2][:, :], AF.Copy),
                     reads=[B_pb[2]], writes=[B_z1[mi]], inc=True)

            pend_h = None
            for mb in range(8):
                for mi in range(8):
                    m = mb * 8 + mi
                    halves = []
                    for half in range(2):
                        src = s_w1[m][:, 1024 * half:1024 * half + 1024]
                        s1, b1 = stream("w1", 3, B_w1, lambda slot, src=src: (
                            lambda s: s.dma_start(
                                out=w1b[:, slot * 1024:(slot + 1) * 1024], in_=src)),
                            dep=B_sw1[m // 4])
                        src3 = s_w3[m][:, 1024 * half:1024 * half + 1024]
                        s3, b3 = stream("w3", 3, B_w3, lambda slot, src3=src3: (
                            lambda s: s.dma_start(
                                out=w3b[:, slot * 1024:(slot + 1) * 1024], in_=src3)),
                            dep=B_sw3[m // 4])
                        halves.append((s1, b1, s3, b3))
                    itemsG, itemsL = [], []
                    for d in range(DC):
                        s1, b1, s3, b3 = halves[d // 8]
                        dd = d % 8
                        itemsG.append((pb[0][:, :],
                                       w1b[:, s1 * 1024 + dd * 128: s1 * 1024 + dd * 128 + 128],
                                       xln[:, d * CH:(d + 1) * CH], [b1, B_xln[d]]))
                        itemsL.append((pb[1][:, :],
                                       w3b[:, s3 * 1024 + dd * 128: s3 * 1024 + dd * 128 + 128],
                                       xln[:, d * CH:(d + 1) * CH], [b3, B_xln[d]]))
                    mm_group(0, itemsG)
                    mm_group(1, itemsL)
                    if pend_h is not None:
                        emit_h128(*pend_h)
                    sp = m % 2
                    hp = m % 2
                    emit("act",
                         lambda s, sp=sp: s.activation(
                             silu_sb[:, sp * CH:(sp + 1) * CH], pb[0][:, :], AF.Silu),
                         reads=[B_pb[0]], writes=[B_silu[sp]], inc=True)
                    emit("dve",
                         lambda v, sp=sp, hp=hp: v.tensor_mul(
                             hbf[:, hp * CH:(hp + 1) * CH],
                             silu_sb[:, sp * CH:(sp + 1) * CH], pb[1][:, :]),
                         reads=[B_silu[sp], B_pb[1]], writes=[B_hbf[hp]], inc=True)
                    pend_h = (hp, mi)
                if pend_h is not None:
                    emit_h128(*pend_h)
                    pend_h = None
                for g in range(4):
                    slabs = []
                    for oi in range(4):
                        o = 4 * g + oi
                        src = s_g1p[mb, o]
                        sg, bg = stream("g1", 6, B_g1, lambda slot, src=src: (
                            lambda s: s.dma_start(
                                out=g1b[:, slot * 1024:(slot + 1) * 1024], in_=src)),
                            dep=B_sg1p[mb])
                        slabs.append((sg, bg))
                    for oi in range(4):
                        o = 4 * g + oi
                        sg, bg = slabs[oi]
                        bank = 4 + oi
                        items = []
                        for mi in range(8):
                            off = sg * 1024 + mi * 128
                            items.append((pb[bank][:, :], g1b[:, off:off + 128],
                                          z1r[:, mi * CH:(mi + 1) * CH],
                                          [bg, B_z1[mi]]))
                        mm_group(bank, items)
                    for oi in range(4):
                        o = 4 * g + oi
                        bank = 4 + oi
                        emit("dve",
                             lambda v, o=o, bank=bank: v.tensor_add(
                                 xbuf[:, o * CH:(o + 1) * CH],
                                 xbuf[:, o * CH:(o + 1) * CH], pb[bank][:, :]),
                             reads=[B_x[o], B_pb[bank]], writes=[B_x[o]], inc=True)
                        if tile_cb is not None and mb == 7:
                            tile_cb(o)

        def attention(ci, tile_cb=None):
            cur = ci % 2
            prev = 1 - cur
            kcur, kprev = kc[cur], kc[prev]
            vcur, vprev = vc[cur], vc[prev]
            Bkc, Bkp = B_kc[cur], B_kc[prev]
            Bvc, Bvp = B_vc[cur], B_vc[prev]

            for g in range(4):
                for d in range(DC):
                    src = s_wv[g, d]
                    sv, bv = stream("wv", 3, B_wv, lambda slot, src=src: (
                        lambda s: s.dma_start(
                            out=wvb[:, slot * 512:(slot + 1) * 512], in_=src)),
                        dep=B_swv)
                    for ts in range(4):
                        bank = 4 + ts
                        first = d == 0
                        last = d == DC - 1
                        emit("pe",
                             lambda t, sv=sv, d=d, ts=ts, bank=bank, first=first, last=last:
                                 t.matmul(pb[bank][:, :],
                                          xln[:, d * CH + ts * 128: d * CH + ts * 128 + 128],
                                          wvb[:, sv * 512:(sv + 1) * 512],
                                          start=first, stop=last),
                             reads=[bv, B_xln[d]],
                             first_writes=[B_pb[bank]] if first else (),
                             writes=[B_pb[bank]] if last else (),
                             inc=last)
                for ts in range(4):
                    bank = 4 + ts
                    emit("act",
                         lambda s, g=g, ts=ts, bank=bank, vcur=vcur: s.activation(
                             vcur[:, ts * D + g * 512: ts * D + g * 512 + 512],
                             pb[bank][:, :], AF.Copy),
                         reads=[B_pb[bank]], writes=[Bvc], inc=True)

            o_bf = arena[:, 0:NH * CH]
            qs_ = [arena[:, 8192:8704], arena[:, 8704:9216]]
            expb = [arena[:, 9216:9728], arena[:, 9728:10240]]
            p_s = [arena[:, 10240:10752], arena[:, 10752:11264]]

            for h in range(NH):
                qh, kh = [], []
                for half in range(2):
                    srcq = s_qk[h][:, 1024 * half:1024 * half + 1024]
                    sq, bq = stream("qk_q", 2, B_qb, lambda slot, srcq=srcq: (
                        lambda s: s.dma_start(
                            out=qb[:, slot * 1024:(slot + 1) * 1024], in_=srcq)),
                        dep=B_sqk[h // 4])
                    qh.append((sq, bq))
                    srck = s_qk[16 + h][:, 1024 * half:1024 * half + 1024]
                    sk, bk = stream("qk_k", 2, B_kb, lambda slot, srck=srck: (
                        lambda s: s.dma_start(
                            out=kb[:, slot * 1024:(slot + 1) * 1024], in_=srck)),
                        dep=B_sqk[4 + h // 4])
                    kh.append((sk, bk))
                items = []
                for d in range(DC):
                    sk, bk = kh[d // 8]
                    dd = d % 8
                    items.append((pb[1][:, :],
                                  kb[:, sk * 1024 + dd * 128: sk * 1024 + dd * 128 + 128],
                                  xln[:, d * CH:(d + 1) * CH], [bk, B_xln[d]]))
                mm_group(1, items)
                emit("act",
                     lambda s, h=h, kcur=kcur: s.activation(
                         kcur[:, h * CH:(h + 1) * CH], pb[1][:, :], AF.Copy),
                     reads=[B_pb[1]], writes=[Bkc], inc=True)
                if ci == 0:
                    continue
                items = []
                for d in range(DC):
                    sq, bq = qh[d // 8]
                    dd = d % 8
                    items.append((pb[0][:, :],
                                  qb[:, sq * 1024 + dd * 128: sq * 1024 + dd * 128 + 128],
                                  xln[:, d * CH:(d + 1) * CH], [bq, B_xln[d]]))
                mm_group(0, items)
                q_bf = qs_[h % 2]
                B_q = B_qbf[h % 2]
                emit("act",
                     lambda s, q_bf=q_bf: s.activation(q_bf, pb[0][:, :], AF.Copy),
                     reads=[B_pb[0]], writes=[B_q], inc=True)

                for ti, t8 in enumerate((3, 4, 0, 1, 2, 5, 6, 7)):
                    lo, hi, w = ST_LO[t8], ST_HI[t8], ST_W[t8]
                    moff = ST_OFF[t8]
                    stp = 2 + (ti % 2)
                    ksrc = kprev if t8 < 4 else kcur
                    Bks = Bkp if t8 < 4 else Bkc
                    koff = h * CH + (t8 % 4) * 128
                    emit("pe",
                         lambda t, stp=stp, ksrc=ksrc, koff=koff, lo=lo, hi=hi, q_bf=q_bf:
                             t.matmul(pb[stp][:, 0:hi - lo],
                                      ksrc[:, koff:koff + 128],
                                      q_bf[:, lo:hi], start=True, stop=True),
                         reads=[Bks, B_q], first_writes=[B_pb[stp]],
                         writes=[B_pb[stp]], inc=True)
                    ep = t8 % 2
                    pp = ti % 2
                    p0 = p_s[pp]
                    emit("act",
                         lambda s, stp=stp, ep=ep, w=w: s.activation(
                             expb[ep][:, 0:w], pb[stp][:, 0:w], AF.Exp,
                             bias=ebias[:, 0:1], scale=SCALE),
                         reads=[B_pb[stp], B_ebias], writes=[B_exp[ep]], inc=True)
                    emit("dve",
                         lambda v, ep=ep, w=w, moff=moff, p0=p0: v.tensor_mul(
                             p0[:, 0:w], expb[ep][:, 0:w], maskb[:, moff:moff + w]),
                         reads=[B_exp[ep], B_mask], writes=[B_pbf[pp]], inc=True)
                    emit("pe",
                         lambda t, lo=lo, w=w, ti=ti, p0=p0: t.matmul(
                             pb[4][:, lo:lo + w], onesM[:, :], p0[:, 0:w],
                             start=(ti == 0), stop=(ti == 7)),
                         reads=[B_pbf[pp], B_onesM],
                         first_writes=[B_pb[4]] if ti == 0 else (),
                         writes=[B_pb[4]] if ti == 7 else (),
                         inc=True)
                    vsrc = vprev if t8 < 4 else vcur
                    Bvs = Bvp if t8 < 4 else Bvc
                    voff = (t8 % 4) * D + h * 128
                    emit("pe",
                         lambda t, vsrc=vsrc, voff=voff, lo=lo, w=w, ti=ti, p0=p0: t.matmul(
                             pb[5][:, lo:lo + w], vsrc[:, voff:voff + 128], p0[:, 0:w],
                             start=(ti == 0), stop=(ti == 7)),
                         reads=[B_pbf[pp], Bvs],
                         first_writes=[B_pb[5]] if ti == 0 else (),
                         writes=[B_pb[5]] if ti == 7 else (),
                         inc=True)
                rp = h % 2
                emit("dve",
                     lambda v, rp=rp: v.tensor_copy(rbsb[:, rp * CH:(rp + 1) * CH],
                                                    pb[4][:, :]),
                     reads=[B_pb[4]], writes=[B_rb[rp]], inc=True)
                emit("dve",
                     lambda v, rp=rp: v.reciprocal(rbsb[:, rp * CH:(rp + 1) * CH],
                                                   rbsb[:, rp * CH:(rp + 1) * CH]),
                     reads=[B_rb[rp]], writes=[B_rb[rp]], inc=True)
                emit("dve",
                     lambda v, h=h, rp=rp: v.tensor_mul(
                         o_bf[:, h * CH:(h + 1) * CH],
                         rbsb[:, rp * CH:(rp + 1) * CH], pb[5][:, :]),
                     reads=[B_rb[rp], B_pb[5]], writes=[B_obf], inc=True)

            if ci == 0:
                return
            for o in range(DC):
                wh = []
                for half in range(2):
                    src = s_wout[o][:, 1024 * half:1024 * half + 1024]
                    so, bo = stream("wout", 3, B_wout, lambda slot, src=src: (
                        lambda s: s.dma_start(
                            out=woutb[:, slot * 1024:(slot + 1) * 1024], in_=src)),
                        dep=B_swout[o // 4])
                    wh.append((so, bo))
                bank = 2 + o % 2
                items = []
                for d in range(DC):
                    so, bo = wh[d // 8]
                    dd = d % 8
                    items.append((pb[bank][:, :],
                                  woutb[:, so * 1024 + dd * 128: so * 1024 + dd * 128 + 128],
                                  o_bf[:, d * CH:(d + 1) * CH], [bo, B_obf]))
                mm_group(bank, items)
                emit("dve",
                     lambda v, o=o, bank=bank: v.tensor_add(
                         xbuf[:, o * CH:(o + 1) * CH],
                         xbuf[:, o * CH:(o + 1) * CH], pb[bank][:, :]),
                     reads=[B_x[o], B_pb[bank]], writes=[B_x[o]], inc=True)
                if tile_cb is not None:
                    tile_cb(o)

        for ci in range(NCH_OWN + 1):
            for c in range(DC):
                emit("sp",
                     lambda s, ci=ci, c=c: s.dma_start(
                         out=xbuf[:, c * CH:(c + 1) * CH],
                         in_=xin[c * 128:(c + 1) * 128, ci * CH:(ci + 1) * CH]),
                     writes=[B_x[c]], inc=True, dma=True)
                ln_tile(c)
            ln_finish()
            ffn(tile_cb=ln_tile)
            ln_finish()
            attention(ci, tile_cb=ln_tile if ci > 0 else None)
            if ci == 0:
                continue
            ln_finish()
            ffn()
            for c in range(DC):
                emit("act",
                     lambda s, ci=ci, c=c: s.dma_start(
                         out=out[c * 128:(c + 1) * 128, (ci - 1) * CH:ci * CH],
                         in_=xbuf[:, c * CH:(c + 1) * CH]),
                     reads=[B_x[c]], inc=True, dma=True)
            if ci == 1:
                emit("pool", lambda g: g.dma_start(out=maskb[:, :], in_=mrest),
                     writes=[B_mask], inc=True, dma=True)

        qa_finals = {k: P.cnt[k] for k in SEM_NAMES if k.startswith("qa") and P.cnt[k] > 0}

        def runner(phys, handle):
            for waits, fn, inc_key, amt in P.ops[phys]:
                for k, v in waits.items():
                    handle.wait_ge(sems[k], v)
                if fn is None:
                    if inc_key is not None:
                        handle.nop().then_inc(sems[inc_key], amt)
                    continue
                inst = fn(handle)
                if inc_key is not None:
                    inst.then_inc(sems[inc_key], amt)

        @block.tensor
        def _(t):
            runner("pe", t)

        @block.scalar
        def _(s):
            runner("act", s)
            for k, v in qa_finals.items():
                s.wait_ge(sems[k], v)

        @block.vector
        def _(v):
            runner("dve", v)

        @block.gpsimd
        def _(g):
            runner("pool", g)

        @block.sync
        def _(s):
            runner("sp", s)

    return nc


def kernel(L0_attn_x, L0_attn_w_qkv, L0_attn_w_out, L0_ffn_W1, L0_ffn_W2, L0_ffn_W3):
    x = np.asarray(L0_attn_x, dtype=np.float32)[0]
    w_qkv = np.asarray(L0_attn_w_qkv, dtype=np.float32)
    w_out = np.asarray(L0_attn_w_out, dtype=np.float32)
    W1 = np.asarray(L0_ffn_W1, dtype=np.float32)
    W2 = np.asarray(L0_ffn_W2, dtype=np.float32)
    W3 = np.asarray(L0_ffn_W3, dtype=np.float32)

    w1t = np.ascontiguousarray(W1.T)
    w3t = np.ascontiguousarray(W3.T)
    w2t = np.ascontiguousarray(W2.T)
    wqkvt = np.ascontiguousarray(w_qkv.T)
    woutt = np.ascontiguousarray(w_out.T)
    h64 = _hadamard(64)
    h128 = _hadamard(128)
    mrest = _band_mask(False)
    mfirst0 = _band_mask(True)

    nc = build_nc()
    in_maps = []
    for c in range(NCORES):
        r0 = 2048 * c
        if c == 0:
            halo = np.zeros((CH, D), dtype=np.float32)
        else:
            halo = x[r0 - CH:r0]
        xc = np.concatenate([halo, x[r0:r0 + T_OWN]], axis=0)
        xin = np.ascontiguousarray(xc.T)
        in_maps.append({
            "xin": xin,
            "w1t": w1t, "w3t": w3t, "w2t": w2t,
            "wqkvt": wqkvt, "woutt": woutt,
            "h64": h64, "h128": h128,
            "mfirst": mfirst0 if c == 0 else mrest,
            "mrest": mrest,
        })

    res = run_bass_kernel_spmd(nc, in_maps, core_ids=list(range(NCORES)))
    outs = []
    for c in range(NCORES):
        o = res.results[c]["out"]
        outs.append(np.ascontiguousarray(o.T))
    full = np.concatenate(outs, axis=0)
    return full.reshape(1, 16384, 2048).astype(np.float32)


# version 13
# speedup vs baseline: 1.0052x; 1.0052x over previous
import contextlib
import math

import numpy as np
import concourse.bass as bass
import concourse.mybir as mybir
from concourse.bass_utils import run_bass_kernel_spmd

F32 = mybir.dt.float32
BF16 = mybir.dt.bfloat16
AF = mybir.ActivationFunctionType
ALU = mybir.AluOpType

NCORES = 8
D = 2048
DC = 16
HID = 8192
HC = 64
NH = 16
HD = 128
CH = 512
NCH_OWN = 4
T_OWN = CH * NCH_OWN
T_CORE = T_OWN + CH
SCALE = 1.0 / math.sqrt(HD)
EXP_BIAS = -30.0
G1_SCALE = 0.5 / math.sqrt(HID)
LN_EPS = 1e-5

ST_LO = [max(0, 128 * t - 512) for t in range(8)]
ST_HI = [min(512, 128 * t + 128) for t in range(8)]
ST_W = [h - l for l, h in zip(ST_LO, ST_HI)]
ST_OFF = [0]
for _t in range(8):
    ST_OFF.append(ST_OFF[-1] + ST_W[_t])
MASK_COLS = ST_OFF[-1]


def _hadamard(n):
    h = np.array([[1.0]], dtype=np.float32)
    while h.shape[0] < n:
        h = np.block([[h, h], [h, -h]])
    return np.ascontiguousarray(h.astype(np.float32))


def _band_mask(first_restrict):
    m = np.zeros((128, MASK_COLS), dtype=np.float32)
    for t in range(8):
        p = np.arange(128)[:, None]
        j = np.arange(ST_W[t])[None, :]
        tk = 128 * t + p
        tq = ST_LO[t] + j
        v = ((tk >= tq) & (tk <= tq + 512)).astype(np.float32)
        if first_restrict and t < 4:
            v[:] = 0.0
        m[:, ST_OFF[t]:ST_OFF[t] + ST_W[t]] = v
    return m


class Buf:
    __slots__ = ("name", "wl", "rl", "pend_r")

    def __init__(self, name):
        self.name = name
        self.wl = {}
        self.rl = {}
        self.pend_r = set()


PHYS = ("pe", "act", "dve", "pool", "sp")
COMP_SEM = {"pe": "pe", "act": "act", "dve": "dve", "pool": "pool", "sp": "sp_c"}
DMA_ROT = {"act": ("qa", 8), "pool": ("qg", 8), "sp": ("qs", 12)}
SEM_NAMES = tuple(["pe", "act", "dve", "pool", "sp_c"]
                  + [f"qa{i}" for i in range(8)]
                  + [f"qg{i}" for i in range(8)]
                  + [f"qs{i}" for i in range(12)])


class Prog:
    def __init__(self):
        self.ops = {p: [] for p in PHYS}
        self.cnt = {s: 0 for s in SEM_NAMES}
        self.waited = {p: {} for p in PHYS}
        self.pend_bufs = {p: [] for p in PHYS}
        self.qrot = {p: 0 for p in PHYS}

    def _resolve(self, phys, evt):
        for pb in self.pend_bufs[phys]:
            if phys in pb.pend_r:
                pb.pend_r.discard(phys)
                k, v = evt
                if pb.rl.get(k, 0) < v:
                    pb.rl[k] = v
        self.pend_bufs[phys] = []

    def _force_inc(self, phys):
        key = COMP_SEM[phys]
        ops = self.ops[phys]
        for i in range(len(ops) - 1, -1, -1):
            waits, fn, ik, amt = ops[i]
            if fn is None:
                continue
            if ik is None:
                self.cnt[key] += 1
                ops[i] = (waits, fn, key, 1)
                self._resolve(phys, (key, self.cnt[key]))
                return
            break
        self.cnt[key] += 1
        ops.append(({}, None, key, 1))
        self._resolve(phys, (key, self.cnt[key]))

    def emit(self, phys, fn, reads=(), writes=(), first_writes=(),
             writes_merge=(), inc=False, dma=False):
        waits = {}
        W = self.waited[phys]

        def need(k, v):
            if phys == "pe" and k == "pe":
                return
            if W.get(k, 0) >= v:
                return
            if waits.get(k, 0) < v:
                waits[k] = v

        for b in reads:
            for k, v in b.wl.items():
                need(k, v)
        for b in tuple(writes) + tuple(first_writes):
            for k, v in b.wl.items():
                need(k, v)
            for p in tuple(b.pend_r):
                self._force_inc(p)
            for k, v in b.rl.items():
                need(k, v)
        for b in writes_merge:
            for p in tuple(b.pend_r):
                self._force_inc(p)
            for k, v in b.rl.items():
                need(k, v)

        if dma:
            pref, R = DMA_ROT[phys]
            semkey = f"{pref}{self.qrot[phys] % R}"
            self.qrot[phys] += 1
            amt = 16
            if self.cnt[semkey] > 0:
                need(semkey, self.cnt[semkey])
        else:
            semkey = COMP_SEM[phys]
            amt = 1
        if inc:
            self.cnt[semkey] += amt
            evt = (semkey, self.cnt[semkey])
        else:
            evt = None

        self.ops[phys].append((waits, fn, semkey if inc else None, amt))
        for k, v in waits.items():
            if W.get(k, 0) < v:
                W[k] = v

        for b in reads:
            if evt is not None:
                if b.rl.get(evt[0], 0) < evt[1]:
                    b.rl[evt[0]] = evt[1]
                b.pend_r.discard(phys)
            else:
                if phys not in b.pend_r:
                    b.pend_r.add(phys)
                    self.pend_bufs[phys].append(b)
        if inc and not dma:
            self._resolve(phys, evt)
        for b in writes:
            assert inc, f"write to {b.name} without inc"
            b.wl = {evt[0]: evt[1]}
            b.rl = {}
            b.pend_r = set()
        for b in writes_merge:
            assert inc, f"merge-write to {b.name} without inc"
            if b.wl.get(evt[0], 0) < evt[1]:
                b.wl[evt[0]] = evt[1]


def build_nc():
    nc = bass.Bass("TRN2", target_bir_lowering=False, debug=False)

    xin = nc.dram_tensor("xin", [D, T_CORE], F32, kind="ExternalInput").ap()
    w1t = nc.dram_tensor("w1t", [D, HID], F32, kind="ExternalInput").ap()
    w3t = nc.dram_tensor("w3t", [D, HID], F32, kind="ExternalInput").ap()
    w2t = nc.dram_tensor("w2t", [HID, D], F32, kind="ExternalInput").ap()
    wqkvt = nc.dram_tensor("wqkvt", [D, 3 * D], F32, kind="ExternalInput").ap()
    woutt = nc.dram_tensor("woutt", [D, D], F32, kind="ExternalInput").ap()
    h64_in = nc.dram_tensor("h64", [64, 64], F32, kind="ExternalInput").ap()
    h128_in = nc.dram_tensor("h128", [128, 128], F32, kind="ExternalInput").ap()
    mfirst = nc.dram_tensor("mfirst", [128, MASK_COLS], F32, kind="ExternalInput").ap()
    mrest = nc.dram_tensor("mrest", [128, MASK_COLS], F32, kind="ExternalInput").ap()
    out = nc.dram_tensor("out", [D, T_OWN], F32, kind="ExternalOutput").ap()

    s_w1 = nc.dram_tensor("s_w1", [HC, 128, 2048], BF16).ap()
    s_w3 = nc.dram_tensor("s_w3", [HC, 128, 2048], BF16).ap()
    s_g1t = nc.dram_tensor("s_g1t", [HC, DC, 128, 128], BF16).ap()
    s_g1p = nc.dram_tensor("s_g1p", [8, DC, 128, 1024], BF16).ap()
    s_qk = nc.dram_tensor("s_qk", [32, 128, 2048], BF16).ap()
    s_wv = nc.dram_tensor("s_wv", [4, DC, 128, 512], BF16).ap()
    s_wout = nc.dram_tensor("s_wout", [DC, 128, 2048], BF16).ap()

    P = Prog()
    emit = P.emit

    with contextlib.ExitStack() as ctx:
        ent = ctx.enter_context

        xbuf = ent(nc.sbuf_tensor("xbuf", [128, DC * CH], F32))
        xln = ent(nc.sbuf_tensor("xln", [128, DC * CH], BF16))
        arena = ent(nc.sbuf_tensor("arena", [128, 11264], BF16))
        z1r = ent(nc.sbuf_tensor("z1r", [128, 8 * CH], BF16))
        silu_sb = ent(nc.sbuf_tensor("silu_sb", [128, 2 * CH], F32))
        hbf = ent(nc.sbuf_tensor("hbf", [128, 2 * CH], BF16))
        kc = [ent(nc.sbuf_tensor("kc0", [128, NH * CH], BF16)),
              ent(nc.sbuf_tensor("kc1", [128, NH * CH], BF16))]
        vc = [ent(nc.sbuf_tensor("vc0", [128, 4 * D], BF16)),
              ent(nc.sbuf_tensor("vc1", [128, 4 * D], BF16))]
        w1b = ent(nc.sbuf_tensor("w1b", [128, 3 * 1024], BF16))
        w3b = ent(nc.sbuf_tensor("w3b", [128, 3 * 1024], BF16))
        g1b = ent(nc.sbuf_tensor("g1b", [128, 6 * 1024], BF16))
        qb = ent(nc.sbuf_tensor("qb", [128, 2 * 1024], BF16))
        kb = ent(nc.sbuf_tensor("kb", [128, 2 * 1024], BF16))
        wvb = ent(nc.sbuf_tensor("wvb", [128, 3 * 512], BF16))
        woutb = ent(nc.sbuf_tensor("woutb", [128, 3 * 1024], BF16))
        maskb = ent(nc.sbuf_tensor("maskb", [128, MASK_COLS], BF16))
        xsqb = ent(nc.sbuf_tensor("xsqb", [128, 2 * CH], BF16))
        xbfb = ent(nc.sbuf_tensor("xbfb", [128, 2 * CH], BF16))
        rbsb = ent(nc.sbuf_tensor("rbsb", [128, 2 * CH], F32))
        stats = ent(nc.sbuf_tensor("stats", [128, 2 * CH], F32))
        h64b = ent(nc.sbuf_tensor("h64b", [128, 64], BF16))
        h128b = ent(nc.sbuf_tensor("h128b", [128, 128], BF16))
        onesM = ent(nc.sbuf_tensor("onesM", [128, 128], BF16))
        ebias = ent(nc.sbuf_tensor("ebias", [128, 1], F32))

        pb4 = [ent(nc.psum_tensor(f"pb{i}", [128, 512], F32)) for i in range(4)]
        pq = ent(nc.psum_tensor("pq", [128, 2048], F32))
        pb = pb4 + [pq[:, 512 * i:512 * i + 512] for i in range(4)]

        sems = {n: ent(nc.semaphore(f"s_{n}")) for n in SEM_NAMES}
        block = ent(nc.Block())

        B_x = [Buf(f"x{i}") for i in range(DC)]
        B_xln = [Buf(f"xln{i}") for i in range(DC)]
        B_obf = Buf("obf")
        B_qbf = [Buf("qbf0"), Buf("qbf1")]
        B_exp = [Buf("exp0"), Buf("exp1")]
        B_pbf = [Buf("pbf0"), Buf("pbf1")]
        B_z1 = [Buf(f"z1_{i}") for i in range(8)]
        B_silu = [Buf(f"silu{i}") for i in range(2)]
        B_hbf = [Buf(f"hbf{i}") for i in range(2)]
        B_kc = [Buf("kc0"), Buf("kc1")]
        B_vc = [Buf("vc0"), Buf("vc1")]
        B_w1 = [Buf(f"w1b{i}") for i in range(3)]
        B_w3 = [Buf(f"w3b{i}") for i in range(3)]
        B_g1 = [Buf(f"g1b{i}") for i in range(6)]
        B_qb = [Buf(f"qb{i}") for i in range(2)]
        B_kb = [Buf(f"kb{i}") for i in range(2)]
        B_wv = [Buf(f"wvb{i}") for i in range(3)]
        B_wout = [Buf(f"woutb{i}") for i in range(3)]
        B_mask = Buf("maskb")
        B_xsq = [Buf(f"xsq{i}") for i in range(2)]
        B_xbf = [Buf(f"xbf{i}") for i in range(2)]
        B_rb = [Buf(f"rb{i}") for i in range(2)]
        B_stats = [Buf(f"st{i}") for i in range(3)]
        B_pb = [Buf(f"pb{i}") for i in range(8)]
        B_h128 = Buf("h128")
        B_onesM = Buf("onesM")
        B_ebias = Buf("ebias")
        B_sw1 = [Buf(f"sw1_{i}") for i in range(16)]
        B_sw3 = [Buf(f"sw3_{i}") for i in range(16)]
        B_sg1 = Buf("sg1")
        B_sg1p = [Buf(f"sg1p_{i}") for i in range(8)]
        B_w2cf = [Buf("w2cf0"), Buf("w2cf1")]
        B_w2cb = [Buf("w2cb0"), Buf("w2cb1")]
        B_h64 = Buf("h64")
        B_sqk = [Buf(f"sqk_{i}") for i in range(8)]
        B_swv = Buf("swv")
        B_swout = [Buf(f"swout_{i}") for i in range(4)]
        B_g1st = [Buf(f"g1s{i}") for i in range(2)]

        emit("pool", lambda g: g.dma_start(out=h64b[64:128, :], in_=h64_in),
             writes=[B_h64], inc=True, dma=True)
        emit("pool", lambda g: g.dma_start(out=h64b[0:64, :], in_=h64_in),
             writes_merge=[B_h64], inc=True, dma=True)
        emit("pool", lambda g: g.dma_start(out=h128b[:, :], in_=h128_in),
             writes=[B_h128], inc=True, dma=True)
        emit("pool", lambda g: g.dma_start(out=maskb[:, :], in_=mfirst),
             writes=[B_mask], inc=True, dma=True)
        emit("dve", lambda v: v.memset(onesM[:, :], 1.0), writes=[B_onesM], inc=True)
        emit("dve", lambda v: v.memset(ebias[:, :], EXP_BIAS), writes=[B_ebias], inc=True)

        w1v = w1t.rearrange("(c p) (m f) -> m p c f", p=128, f=128)
        w3v = w3t.rearrange("(c p) (m f) -> m p c f", p=128, f=128)
        qkv_v = wqkvt[:, 0:4096].rearrange("(c p) (j f) -> j p c f", p=128, f=128)
        wv_v = wqkvt[:, 4096:6144].rearrange("(c p) (g f) -> g c p f", p=128, f=512)
        wout_v = woutt.rearrange("(c p) (j f) -> j p c f", p=128, f=128)

        def sub_cast(dst3, src3):
            return lambda g, dst3=dst3, src3=src3: g.dma_start(out=dst3, in_=src3)

        def w_sub(sten, view, m):
            return sub_cast(sten[m].rearrange("p (c f) -> p c f", f=128), view[m])

        w1_jobs = [[B_sw1[i]] + [w_sub(s_w1, w1v, 4 * i + j) for j in range(4)]
                   for i in range(16)]
        w3_jobs = [[B_sw3[i]] + [w_sub(s_w3, w3v, 4 * i + j) for j in range(4)]
                   for i in range(16)]
        tail_jobs = []
        for i in range(8):
            tail_jobs.append([B_sqk[i]] + [w_sub(s_qk, qkv_v, 4 * i + j) for j in range(4)])
        tail_jobs.append([B_swv] + [
            (lambda g, j=j: g.dma_start(out=s_wv[j:j + 1], in_=wv_v[j:j + 1]))
            for j in range(4)])
        for i in range(4):
            tail_jobs.append([B_swout[i]] + [w_sub(s_wout, wout_v, 4 * i + j)
                                             for j in range(4)])

        def emit_cast_job(job):
            b = job[0]
            for fn in job[1:]:
                emit("pool", fn, writes_merge=[b], inc=True, dma=True)

        for job in (w1_jobs[0], w3_jobs[0], w1_jobs[1], w3_jobs[1]):
            emit_cast_job(job)

        w2v = w2t.rearrange("(a b) o -> b a o", b=128)
        w2cf = [arena[64:128, 0:4096].bitcast(F32),
                arena[0:64, 0:4096].bitcast(F32)]
        w2cb = [arena[64:128, 8192:10240], arena[0:64, 8192:10240]]
        h64h = [h64b[64:128, :], h64b[0:64, :]]
        g1st = [arena[0:64, 4096:6144], arena[0:64, 6144:8192]]

        def g1_store(bidx):
            p2 = bidx % 2
            g1st_v = g1st[p2].rearrange("p (o f) -> p o f", f=128)
            emit("act",
                 lambda s, g1st_v=g1st_v, bidx=bidx: s.dma_start(
                     out=s_g1t[:, :, bidx, :], in_=g1st_v),
                 reads=[B_g1st[p2]], writes=[B_sg1], inc=True, dma=True)

        for bidx in range(128):
            par = bidx % 2
            p2 = bidx % 2
            emit("sp",
                 lambda s, bidx=bidx, par=par: s.dma_start(out=w2cf[par], in_=w2v[bidx]),
                 writes=[B_w2cf[par]], inc=True, dma=True)
            ceng = "act" if par == 0 else "dve"
            if ceng == "act":
                emit("act", lambda s, par=par: s.activation(w2cb[par], w2cf[par], AF.Copy),
                     reads=[B_w2cf[par]], writes=[B_w2cb[par]], inc=True)
            else:
                emit("dve", lambda v, par=par: v.tensor_copy(w2cb[par], w2cf[par]),
                     reads=[B_w2cf[par]], writes=[B_w2cb[par]], inc=True)
            boff = 4 * par
            for j in range(4):
                jp = boff + j
                emit("pe",
                     lambda t, par=par, j=j, jp=jp: t.matmul(
                         pb[jp][0:64, :], h64h[par],
                         w2cb[par][:, 512 * j:512 * j + 512],
                         start=True, stop=True),
                     reads=[B_w2cb[par], B_h64], first_writes=[B_pb[jp]],
                     writes=[B_pb[jp]], inc=True)
            for j in range(4):
                jb = boff + j
                eng = "dve" if (bidx + j) % 2 == 0 else "act"
                if eng == "act":
                    emit("act",
                         lambda s, p2=p2, j=j, jb=jb: s.activation(
                             g1st[p2][:, 512 * j:512 * j + 512],
                             pb[jb][0:64, :], AF.Copy, scale=G1_SCALE),
                         reads=[B_pb[jb]],
                         writes_merge=[B_g1st[p2]], inc=True)
                else:
                    emit("dve",
                         lambda v, p2=p2, j=j, jb=jb: v.tensor_scalar(
                             g1st[p2][:, 512 * j:512 * j + 512],
                             pb[jb][0:64, :], G1_SCALE, None, ALU.mult),
                         reads=[B_pb[jb]],
                         writes_merge=[B_g1st[p2]], inc=True)
            if bidx >= 1:
                g1_store(bidx - 1)
        g1_store(127)

        for i in range(2, 16):
            emit_cast_job(w1_jobs[i])
            emit_cast_job(w3_jobs[i])

        for mb in range(8):
            for o in range(DC):
                rsrc = s_g1t[8 * mb:8 * mb + 8, o].rearrange("m p f -> p m f")
                rdst = s_g1p[mb, o].rearrange("p (m f) -> p m f", f=128)
                q = "act" if (mb * DC + o) % 2 else "sp"
                emit(q, lambda s, rsrc=rsrc, rdst=rdst: s.dma_start(out=rdst, in_=rsrc),
                     reads=[B_sg1], writes_merge=[B_sg1p[mb]], inc=True, dma=True)

        for job in tail_jobs:
            emit_cast_job(job)

        emit("dve", lambda v: v.memset(arena[0:1, 0:1], 0.0),
             writes=[B_w2cf[0], B_w2cf[1], B_w2cb[0], B_w2cb[1],
                     B_g1st[0], B_g1st[1],
                     B_obf, B_qbf[0], B_qbf[1], B_exp[0], B_exp[1],
                     B_pbf[0], B_pbf[1]],
             inc=True)

        def mm_group(bank, items):
            n = len(items)
            for i, (reg, lh, rh, reads) in enumerate(items):
                first = i == 0
                last = i == n - 1
                emit("pe",
                     lambda t, reg=reg, lh=lh, rh=rh, first=first, last=last:
                         t.matmul(reg, lh, rh, start=first, stop=last),
                     reads=list(reads),
                     first_writes=[B_pb[bank]] if first else (),
                     writes=[B_pb[bank]] if last else (),
                     inc=last)

        def ln_tile(c):
            par = c % 2
            emit("act",
                 lambda s, c=c, par=par: s.activation(
                     xsqb[:, par * CH:(par + 1) * CH],
                     xbuf[:, c * CH:(c + 1) * CH], AF.Square),
                 reads=[B_x[c]], writes=[B_xsq[par]], inc=True)
            emit("dve",
                 lambda v, c=c, par=par: v.tensor_copy(
                     xbfb[:, par * CH:(par + 1) * CH],
                     xbuf[:, c * CH:(c + 1) * CH]),
                 reads=[B_x[c]], writes=[B_xbf[par]], inc=True)
            emit("pe",
                 lambda t, c=c, par=par: t.matmul(
                     pb[0][:, :], onesM[:, :],
                     xbfb[:, par * CH:(par + 1) * CH],
                     start=(c == 0), stop=(c == DC - 1)),
                 reads=[B_xbf[par], B_onesM],
                 first_writes=[B_pb[0]] if c == 0 else (),
                 writes=[B_pb[0]] if c == DC - 1 else (),
                 inc=True)
            emit("pe",
                 lambda t, c=c, par=par: t.matmul(
                     pb[1][:, :], onesM[:, :],
                     xsqb[:, par * CH:(par + 1) * CH],
                     start=(c == 0), stop=(c == DC - 1)),
                 reads=[B_xsq[par], B_onesM],
                 first_writes=[B_pb[1]] if c == 0 else (),
                 writes=[B_pb[1]] if c == DC - 1 else (),
                 inc=True)

        def ln_finish():
            sA = stats[:, 0:CH]
            sB = stats[:, CH:2 * CH]
            sC = rbsb[:, 0:CH]
            B_stats[2] = B_rb[0]
            emit("dve", lambda v: v.tensor_scalar(sA, pb[0][:, :], -1.0 / D, None, ALU.mult),
                 reads=[B_pb[0]], writes=[B_stats[0]], inc=True)
            emit("dve", lambda v: v.tensor_scalar(sB, pb[1][:, :], 1.0 / D, LN_EPS, ALU.mult, ALU.add),
                 reads=[B_pb[1]], writes=[B_stats[1]], inc=True)
            emit("dve", lambda v: v.tensor_mul(sC, sA, sA),
                 reads=[B_stats[0]], writes=[B_stats[2]], inc=True)
            emit("dve", lambda v: v.tensor_sub(sB, sB, sC),
                 reads=[B_stats[1], B_stats[2]], writes=[B_stats[1]], inc=True)
            emit("act", lambda s: s.activation(sC, sB, AF.Ln),
                 reads=[B_stats[1]], writes=[B_stats[2]], inc=True)
            emit("act", lambda s: s.activation(sB, sC, AF.Exp, scale=-0.5),
                 reads=[B_stats[2]], writes=[B_stats[1]], inc=True)
            for c in range(DC):
                par = c % 2
                eng = "dve"
                if eng == "dve":
                    emit("dve",
                         lambda v, c=c, par=par: v.tensor_add(
                             silu_sb[:, par * CH:(par + 1) * CH],
                             xbuf[:, c * CH:(c + 1) * CH], sA),
                         reads=[B_x[c], B_stats[0]], writes=[B_silu[par]], inc=True)
                    emit("dve",
                         lambda v, c=c, par=par: v.tensor_mul(
                             xln[:, c * CH:(c + 1) * CH],
                             silu_sb[:, par * CH:(par + 1) * CH], sB),
                         reads=[B_silu[par], B_stats[1]], writes=[B_xln[c]], inc=True)
                else:
                    emit("pool",
                         lambda g, c=c, par=par: g.tensor_add(
                             silu_sb[:, par * CH:(par + 1) * CH],
                             xbuf[:, c * CH:(c + 1) * CH], sA),
                         reads=[B_x[c], B_stats[0]], writes=[B_silu[par]], inc=True)
                    emit("pool",
                         lambda g, c=c, par=par: g.tensor_mul(
                             xln[:, c * CH:(c + 1) * CH],
                             silu_sb[:, par * CH:(par + 1) * CH], sB),
                         reads=[B_silu[par], B_stats[1]], writes=[B_xln[c]], inc=True)

        ring_state = {}

        def stream(name, rings, bufs, fn_maker, dep=None):
            idx = ring_state.get(name, 0)
            ring_state[name] = idx + 1
            slot = idx % rings
            b = bufs[slot]
            emit("sp", fn_maker(slot), reads=[dep] if dep is not None else [],
                 writes=[b], inc=True, dma=True)
            return slot, b

        def ffn(tile_cb=None):
            def emit_h128(hp, mi):
                emit("pe",
                     lambda t, hp=hp: t.matmul(
                         pb[2][:, :], h128b[:, :], hbf[:, hp * CH:(hp + 1) * CH],
                         start=True, stop=True),
                     reads=[B_hbf[hp], B_h128], first_writes=[B_pb[2]],
                     writes=[B_pb[2]], inc=True)
                emit("act",
                     lambda s, mi=mi: s.activation(
                         z1r[:, mi * CH:(mi + 1) * CH], pb[2][:, :], AF.Copy),
                     reads=[B_pb[2]], writes=[B_z1[mi]], inc=True)

            pend_h = None
            for mb in range(8):
                for mi in range(8):
                    m = mb * 8 + mi
                    halves = []
                    for half in range(2):
                        src = s_w1[m][:, 1024 * half:1024 * half + 1024]
                        s1, b1 = stream("w1", 3, B_w1, lambda slot, src=src: (
                            lambda s: s.dma_start(
                                out=w1b[:, slot * 1024:(slot + 1) * 1024], in_=src)),
                            dep=B_sw1[m // 4])
                        src3 = s_w3[m][:, 1024 * half:1024 * half + 1024]
                        s3, b3 = stream("w3", 3, B_w3, lambda slot, src3=src3: (
                            lambda s: s.dma_start(
                                out=w3b[:, slot * 1024:(slot + 1) * 1024], in_=src3)),
                            dep=B_sw3[m // 4])
                        halves.append((s1, b1, s3, b3))
                    itemsG, itemsL = [], []
                    for d in range(DC):
                        s1, b1, s3, b3 = halves[d // 8]
                        dd = d % 8
                        itemsG.append((pb[0][:, :],
                                       w1b[:, s1 * 1024 + dd * 128: s1 * 1024 + dd * 128 + 128],
                                       xln[:, d * CH:(d + 1) * CH], [b1, B_xln[d]]))
                        itemsL.append((pb[1][:, :],
                                       w3b[:, s3 * 1024 + dd * 128: s3 * 1024 + dd * 128 + 128],
                                       xln[:, d * CH:(d + 1) * CH], [b3, B_xln[d]]))
                    mm_group(0, itemsG)
                    mm_group(1, itemsL)
                    if pend_h is not None:
                        emit_h128(*pend_h)
                    sp = m % 2
                    hp = m % 2
                    emit("act",
                         lambda s, sp=sp: s.activation(
                             silu_sb[:, sp * CH:(sp + 1) * CH], pb[0][:, :], AF.Silu),
                         reads=[B_pb[0]], writes=[B_silu[sp]], inc=True)
                    emit("dve",
                         lambda v, sp=sp, hp=hp: v.tensor_mul(
                             hbf[:, hp * CH:(hp + 1) * CH],
                             silu_sb[:, sp * CH:(sp + 1) * CH], pb[1][:, :]),
                         reads=[B_silu[sp], B_pb[1]], writes=[B_hbf[hp]], inc=True)
                    pend_h = (hp, mi)
                if pend_h is not None:
                    emit_h128(*pend_h)
                    pend_h = None
                for g in range(4):
                    slabs = []
                    for oi in range(4):
                        o = 4 * g + oi
                        src = s_g1p[mb, o]
                        sg, bg = stream("g1", 6, B_g1, lambda slot, src=src: (
                            lambda s: s.dma_start(
                                out=g1b[:, slot * 1024:(slot + 1) * 1024], in_=src)),
                            dep=B_sg1p[mb])
                        slabs.append((sg, bg))
                    for oi in range(4):
                        o = 4 * g + oi
                        sg, bg = slabs[oi]
                        bank = 4 + oi
                        items = []
                        for mi in range(8):
                            off = sg * 1024 + mi * 128
                            items.append((pb[bank][:, :], g1b[:, off:off + 128],
                                          z1r[:, mi * CH:(mi + 1) * CH],
                                          [bg, B_z1[mi]]))
                        mm_group(bank, items)
                        emit("dve",
                             lambda v, o=o, bank=bank: v.tensor_add(
                                 xbuf[:, o * CH:(o + 1) * CH],
                                 xbuf[:, o * CH:(o + 1) * CH], pb[bank][:, :]),
                             reads=[B_x[o], B_pb[bank]], writes=[B_x[o]], inc=True)
                        if tile_cb is not None and mb == 7:
                            tile_cb(o)

        def attention(ci, tile_cb=None):
            cur = ci % 2
            prev = 1 - cur
            kcur, kprev = kc[cur], kc[prev]
            vcur, vprev = vc[cur], vc[prev]
            Bkc, Bkp = B_kc[cur], B_kc[prev]
            Bvc, Bvp = B_vc[cur], B_vc[prev]

            for g in range(4):
                for d in range(DC):
                    src = s_wv[g, d]
                    sv, bv = stream("wv", 3, B_wv, lambda slot, src=src: (
                        lambda s: s.dma_start(
                            out=wvb[:, slot * 512:(slot + 1) * 512], in_=src)),
                        dep=B_swv)
                    for ts in range(4):
                        bank = 4 + ts
                        first = d == 0
                        last = d == DC - 1
                        emit("pe",
                             lambda t, sv=sv, d=d, ts=ts, bank=bank, first=first, last=last:
                                 t.matmul(pb[bank][:, :],
                                          xln[:, d * CH + ts * 128: d * CH + ts * 128 + 128],
                                          wvb[:, sv * 512:(sv + 1) * 512],
                                          start=first, stop=last),
                             reads=[bv, B_xln[d]],
                             first_writes=[B_pb[bank]] if first else (),
                             writes=[B_pb[bank]] if last else (),
                             inc=last)
                for ts in range(4):
                    bank = 4 + ts
                    emit("act",
                         lambda s, g=g, ts=ts, bank=bank, vcur=vcur: s.activation(
                             vcur[:, ts * D + g * 512: ts * D + g * 512 + 512],
                             pb[bank][:, :], AF.Copy),
                         reads=[B_pb[bank]], writes=[Bvc], inc=True)

            o_bf = arena[:, 0:NH * CH]
            qs_ = [arena[:, 8192:8704], arena[:, 8704:9216]]
            expb = [arena[:, 9216:9728], arena[:, 9728:10240]]
            p_s = [arena[:, 10240:10752], arena[:, 10752:11264]]

            for h in range(NH):
                qh, kh = [], []
                for half in range(2):
                    srcq = s_qk[h][:, 1024 * half:1024 * half + 1024]
                    sq, bq = stream("qk_q", 2, B_qb, lambda slot, srcq=srcq: (
                        lambda s: s.dma_start(
                            out=qb[:, slot * 1024:(slot + 1) * 1024], in_=srcq)),
                        dep=B_sqk[h // 4])
                    qh.append((sq, bq))
                    srck = s_qk[16 + h][:, 1024 * half:1024 * half + 1024]
                    sk, bk = stream("qk_k", 2, B_kb, lambda slot, srck=srck: (
                        lambda s: s.dma_start(
                            out=kb[:, slot * 1024:(slot + 1) * 1024], in_=srck)),
                        dep=B_sqk[4 + h // 4])
                    kh.append((sk, bk))
                items = []
                for d in range(DC):
                    sk, bk = kh[d // 8]
                    dd = d % 8
                    items.append((pb[1][:, :],
                                  kb[:, sk * 1024 + dd * 128: sk * 1024 + dd * 128 + 128],
                                  xln[:, d * CH:(d + 1) * CH], [bk, B_xln[d]]))
                mm_group(1, items)
                emit("act",
                     lambda s, h=h, kcur=kcur: s.activation(
                         kcur[:, h * CH:(h + 1) * CH], pb[1][:, :], AF.Copy),
                     reads=[B_pb[1]], writes=[Bkc], inc=True)
                if ci == 0:
                    continue
                items = []
                for d in range(DC):
                    sq, bq = qh[d // 8]
                    dd = d % 8
                    items.append((pb[0][:, :],
                                  qb[:, sq * 1024 + dd * 128: sq * 1024 + dd * 128 + 128],
                                  xln[:, d * CH:(d + 1) * CH], [bq, B_xln[d]]))
                mm_group(0, items)
                q_bf = qs_[h % 2]
                B_q = B_qbf[h % 2]
                emit("act",
                     lambda s, q_bf=q_bf: s.activation(q_bf, pb[0][:, :], AF.Copy),
                     reads=[B_pb[0]], writes=[B_q], inc=True)

                for ti, t8 in enumerate((3, 4, 0, 1, 2, 5, 6, 7)):
                    lo, hi, w = ST_LO[t8], ST_HI[t8], ST_W[t8]
                    moff = ST_OFF[t8]
                    stp = 2 + (ti % 2)
                    ksrc = kprev if t8 < 4 else kcur
                    Bks = Bkp if t8 < 4 else Bkc
                    koff = h * CH + (t8 % 4) * 128
                    emit("pe",
                         lambda t, stp=stp, ksrc=ksrc, koff=koff, lo=lo, hi=hi, q_bf=q_bf:
                             t.matmul(pb[stp][:, 0:hi - lo],
                                      ksrc[:, koff:koff + 128],
                                      q_bf[:, lo:hi], start=True, stop=True),
                         reads=[Bks, B_q], first_writes=[B_pb[stp]],
                         writes=[B_pb[stp]], inc=True)
                    ep = t8 % 2
                    pp = ti % 2
                    p0 = p_s[pp]
                    emit("act",
                         lambda s, stp=stp, ep=ep, w=w: s.activation(
                             expb[ep][:, 0:w], pb[stp][:, 0:w], AF.Exp,
                             bias=ebias[:, 0:1], scale=SCALE),
                         reads=[B_pb[stp], B_ebias], writes=[B_exp[ep]], inc=True)
                    emit("dve",
                         lambda v, ep=ep, w=w, moff=moff, p0=p0: v.tensor_mul(
                             p0[:, 0:w], expb[ep][:, 0:w], maskb[:, moff:moff + w]),
                         reads=[B_exp[ep], B_mask], writes=[B_pbf[pp]], inc=True)
                    emit("pe",
                         lambda t, lo=lo, w=w, ti=ti, p0=p0: t.matmul(
                             pb[4][:, lo:lo + w], onesM[:, :], p0[:, 0:w],
                             start=(ti == 0), stop=(ti == 7)),
                         reads=[B_pbf[pp], B_onesM],
                         first_writes=[B_pb[4]] if ti == 0 else (),
                         writes=[B_pb[4]] if ti == 7 else (),
                         inc=True)
                    vsrc = vprev if t8 < 4 else vcur
                    Bvs = Bvp if t8 < 4 else Bvc
                    voff = (t8 % 4) * D + h * 128
                    emit("pe",
                         lambda t, vsrc=vsrc, voff=voff, lo=lo, w=w, ti=ti, p0=p0: t.matmul(
                             pb[5][:, lo:lo + w], vsrc[:, voff:voff + 128], p0[:, 0:w],
                             start=(ti == 0), stop=(ti == 7)),
                         reads=[B_pbf[pp], Bvs],
                         first_writes=[B_pb[5]] if ti == 0 else (),
                         writes=[B_pb[5]] if ti == 7 else (),
                         inc=True)
                rp = h % 2
                emit("dve",
                     lambda v, rp=rp: v.tensor_copy(rbsb[:, rp * CH:(rp + 1) * CH],
                                                    pb[4][:, :]),
                     reads=[B_pb[4]], writes=[B_rb[rp]], inc=True)
                emit("dve",
                     lambda v, rp=rp: v.reciprocal(rbsb[:, rp * CH:(rp + 1) * CH],
                                                   rbsb[:, rp * CH:(rp + 1) * CH]),
                     reads=[B_rb[rp]], writes=[B_rb[rp]], inc=True)
                emit("dve",
                     lambda v, h=h, rp=rp: v.tensor_mul(
                         o_bf[:, h * CH:(h + 1) * CH],
                         rbsb[:, rp * CH:(rp + 1) * CH], pb[5][:, :]),
                     reads=[B_rb[rp], B_pb[5]], writes=[B_obf], inc=True)

            if ci == 0:
                return
            for o in range(DC):
                wh = []
                for half in range(2):
                    src = s_wout[o][:, 1024 * half:1024 * half + 1024]
                    so, bo = stream("wout", 3, B_wout, lambda slot, src=src: (
                        lambda s: s.dma_start(
                            out=woutb[:, slot * 1024:(slot + 1) * 1024], in_=src)),
                        dep=B_swout[o // 4])
                    wh.append((so, bo))
                bank = 2 + o % 2
                items = []
                for d in range(DC):
                    so, bo = wh[d // 8]
                    dd = d % 8
                    items.append((pb[bank][:, :],
                                  woutb[:, so * 1024 + dd * 128: so * 1024 + dd * 128 + 128],
                                  o_bf[:, d * CH:(d + 1) * CH], [bo, B_obf]))
                mm_group(bank, items)
                emit("dve",
                     lambda v, o=o, bank=bank: v.tensor_add(
                         xbuf[:, o * CH:(o + 1) * CH],
                         xbuf[:, o * CH:(o + 1) * CH], pb[bank][:, :]),
                     reads=[B_x[o], B_pb[bank]], writes=[B_x[o]], inc=True)
                if tile_cb is not None:
                    tile_cb(o)

        for ci in range(NCH_OWN + 1):
            for c in range(DC):
                emit("sp",
                     lambda s, ci=ci, c=c: s.dma_start(
                         out=xbuf[:, c * CH:(c + 1) * CH],
                         in_=xin[c * 128:(c + 1) * 128, ci * CH:(ci + 1) * CH]),
                     writes=[B_x[c]], inc=True, dma=True)
                ln_tile(c)
            ln_finish()
            ffn(tile_cb=ln_tile)
            ln_finish()
            attention(ci, tile_cb=ln_tile if ci > 0 else None)
            if ci == 0:
                continue
            ln_finish()
            ffn()
            for c in range(DC):
                emit("act",
                     lambda s, ci=ci, c=c: s.dma_start(
                         out=out[c * 128:(c + 1) * 128, (ci - 1) * CH:ci * CH],
                         in_=xbuf[:, c * CH:(c + 1) * CH]),
                     reads=[B_x[c]], inc=True, dma=True)
            if ci == 1:
                emit("pool", lambda g: g.dma_start(out=maskb[:, :], in_=mrest),
                     writes=[B_mask], inc=True, dma=True)

        qa_finals = {k: P.cnt[k] for k in SEM_NAMES if k.startswith("qa") and P.cnt[k] > 0}

        def runner(phys, handle):
            for waits, fn, inc_key, amt in P.ops[phys]:
                for k, v in waits.items():
                    handle.wait_ge(sems[k], v)
                if fn is None:
                    if inc_key is not None:
                        handle.nop().then_inc(sems[inc_key], amt)
                    continue
                inst = fn(handle)
                if inc_key is not None:
                    inst.then_inc(sems[inc_key], amt)

        @block.tensor
        def _(t):
            runner("pe", t)

        @block.scalar
        def _(s):
            runner("act", s)
            for k, v in qa_finals.items():
                s.wait_ge(sems[k], v)

        @block.vector
        def _(v):
            runner("dve", v)

        @block.gpsimd
        def _(g):
            runner("pool", g)

        @block.sync
        def _(s):
            runner("sp", s)

    return nc


def kernel(L0_attn_x, L0_attn_w_qkv, L0_attn_w_out, L0_ffn_W1, L0_ffn_W2, L0_ffn_W3):
    x = np.asarray(L0_attn_x, dtype=np.float32)[0]
    w_qkv = np.asarray(L0_attn_w_qkv, dtype=np.float32)
    w_out = np.asarray(L0_attn_w_out, dtype=np.float32)
    W1 = np.asarray(L0_ffn_W1, dtype=np.float32)
    W2 = np.asarray(L0_ffn_W2, dtype=np.float32)
    W3 = np.asarray(L0_ffn_W3, dtype=np.float32)

    w1t = np.ascontiguousarray(W1.T)
    w3t = np.ascontiguousarray(W3.T)
    w2t = np.ascontiguousarray(W2.T)
    wqkvt = np.ascontiguousarray(w_qkv.T)
    woutt = np.ascontiguousarray(w_out.T)
    h64 = _hadamard(64)
    h128 = _hadamard(128)
    mrest = _band_mask(False)
    mfirst0 = _band_mask(True)

    nc = build_nc()
    in_maps = []
    for c in range(NCORES):
        r0 = 2048 * c
        if c == 0:
            halo = np.zeros((CH, D), dtype=np.float32)
        else:
            halo = x[r0 - CH:r0]
        xc = np.concatenate([halo, x[r0:r0 + T_OWN]], axis=0)
        xin = np.ascontiguousarray(xc.T)
        in_maps.append({
            "xin": xin,
            "w1t": w1t, "w3t": w3t, "w2t": w2t,
            "wqkvt": wqkvt, "woutt": woutt,
            "h64": h64, "h128": h128,
            "mfirst": mfirst0 if c == 0 else mrest,
            "mrest": mrest,
        })

    res = run_bass_kernel_spmd(nc, in_maps, core_ids=list(range(NCORES)))
    outs = []
    for c in range(NCORES):
        o = res.results[c]["out"]
        outs.append(np.ascontiguousarray(o.T))
    full = np.concatenate(outs, axis=0)
    return full.reshape(1, 16384, 2048).astype(np.float32)


# version 14
# speedup vs baseline: 1.1130x; 1.1073x over previous
import contextlib
import math

import numpy as np
import concourse.bass as bass
import concourse.mybir as mybir
from concourse.bass_utils import run_bass_kernel_spmd

F32 = mybir.dt.float32
BF16 = mybir.dt.bfloat16
AF = mybir.ActivationFunctionType
ALU = mybir.AluOpType

NCORES = 8
D = 2048
DC = 16
HID = 8192
HC = 64
NH = 16
HD = 128
CH = 512
NCH_OWN = 4
T_OWN = CH * NCH_OWN
T_CORE = T_OWN + CH
SCALE = 1.0 / math.sqrt(HD)
EXP_BIAS = -30.0
G1_SCALE = 0.5 / math.sqrt(HID)
LN_EPS = 1e-5

ST_LO = [max(0, 128 * t - 512) for t in range(8)]
ST_HI = [min(512, 128 * t + 128) for t in range(8)]
ST_W = [h - l for l, h in zip(ST_LO, ST_HI)]
ST_OFF = [0]
for _t in range(8):
    ST_OFF.append(ST_OFF[-1] + ST_W[_t])
MASK_COLS = ST_OFF[-1]


def _hadamard(n):
    h = np.array([[1.0]], dtype=np.float32)
    while h.shape[0] < n:
        h = np.block([[h, h], [h, -h]])
    return np.ascontiguousarray(h.astype(np.float32))


def _band_mask(first_restrict):
    m = np.zeros((128, MASK_COLS), dtype=np.float32)
    for t in range(8):
        p = np.arange(128)[:, None]
        j = np.arange(ST_W[t])[None, :]
        tk = 128 * t + p
        tq = ST_LO[t] + j
        v = ((tk >= tq) & (tk <= tq + 512)).astype(np.float32)
        if first_restrict and t < 4:
            v[:] = 0.0
        m[:, ST_OFF[t]:ST_OFF[t] + ST_W[t]] = v
    return m


class Buf:
    __slots__ = ("name", "wl", "rl", "pend_r")

    def __init__(self, name):
        self.name = name
        self.wl = {}
        self.rl = {}
        self.pend_r = set()


PHYS = ("pe", "act", "dve", "pool", "sp")
COMP_SEM = {"pe": "pe", "act": "act", "dve": "dve", "pool": "pool", "sp": "sp_c"}
DMA_ROT = {"act": ("qa", 8), "pool": ("qg", 8), "sp": ("qs", 12)}
SEM_NAMES = tuple(["pe", "act", "dve", "pool", "sp_c"]
                  + [f"qa{i}" for i in range(8)]
                  + [f"qg{i}" for i in range(8)]
                  + [f"qs{i}" for i in range(12)])


class Prog:
    def __init__(self):
        self.ops = {p: [] for p in PHYS}
        self.cnt = {s: 0 for s in SEM_NAMES}
        self.waited = {p: {} for p in PHYS}
        self.pend_bufs = {p: [] for p in PHYS}
        self.qrot = {p: 0 for p in PHYS}

    def _resolve(self, phys, evt):
        for pb in self.pend_bufs[phys]:
            if phys in pb.pend_r:
                pb.pend_r.discard(phys)
                k, v = evt
                if pb.rl.get(k, 0) < v:
                    pb.rl[k] = v
        self.pend_bufs[phys] = []

    def _force_inc(self, phys):
        key = COMP_SEM[phys]
        ops = self.ops[phys]
        for i in range(len(ops) - 1, -1, -1):
            waits, fn, ik, amt = ops[i]
            if fn is None:
                continue
            if ik is None:
                self.cnt[key] += 1
                ops[i] = (waits, fn, key, 1)
                self._resolve(phys, (key, self.cnt[key]))
                return
            break
        self.cnt[key] += 1
        ops.append(({}, None, key, 1))
        self._resolve(phys, (key, self.cnt[key]))

    def emit(self, phys, fn, reads=(), writes=(), first_writes=(),
             writes_merge=(), inc=False, dma=False):
        waits = {}
        W = self.waited[phys]

        def need(k, v):
            if phys == "pe" and k == "pe":
                return
            if W.get(k, 0) >= v:
                return
            if waits.get(k, 0) < v:
                waits[k] = v

        for b in reads:
            for k, v in b.wl.items():
                need(k, v)
        for b in tuple(writes) + tuple(first_writes):
            for k, v in b.wl.items():
                need(k, v)
            for p in tuple(b.pend_r):
                self._force_inc(p)
            for k, v in b.rl.items():
                need(k, v)
        for b in writes_merge:
            for p in tuple(b.pend_r):
                self._force_inc(p)
            for k, v in b.rl.items():
                need(k, v)

        if dma:
            pref, R = DMA_ROT[phys]
            semkey = f"{pref}{self.qrot[phys] % R}"
            self.qrot[phys] += 1
            amt = 16
            if self.cnt[semkey] > 0:
                need(semkey, self.cnt[semkey])
        else:
            semkey = COMP_SEM[phys]
            amt = 1
        if inc:
            self.cnt[semkey] += amt
            evt = (semkey, self.cnt[semkey])
        else:
            evt = None

        self.ops[phys].append((waits, fn, semkey if inc else None, amt))
        for k, v in waits.items():
            if W.get(k, 0) < v:
                W[k] = v

        for b in reads:
            if evt is not None:
                if b.rl.get(evt[0], 0) < evt[1]:
                    b.rl[evt[0]] = evt[1]
                b.pend_r.discard(phys)
            else:
                if phys not in b.pend_r:
                    b.pend_r.add(phys)
                    self.pend_bufs[phys].append(b)
        if inc and not dma:
            self._resolve(phys, evt)
        for b in writes:
            assert inc, f"write to {b.name} without inc"
            b.wl = {evt[0]: evt[1]}
            b.rl = {}
            b.pend_r = set()
        for b in writes_merge:
            assert inc, f"merge-write to {b.name} without inc"
            if b.wl.get(evt[0], 0) < evt[1]:
                b.wl[evt[0]] = evt[1]


def build_nc():
    nc = bass.Bass("TRN2", target_bir_lowering=False, debug=False)

    xin = nc.dram_tensor("xin", [D, T_CORE], F32, kind="ExternalInput").ap()
    w1t = nc.dram_tensor("w1t", [D, HID], F32, kind="ExternalInput").ap()
    w3t = nc.dram_tensor("w3t", [D, HID], F32, kind="ExternalInput").ap()
    w2t = nc.dram_tensor("w2t", [HID, D], F32, kind="ExternalInput").ap()
    wqkvt = nc.dram_tensor("wqkvt", [D, 3 * D], F32, kind="ExternalInput").ap()
    woutt = nc.dram_tensor("woutt", [D, D], F32, kind="ExternalInput").ap()
    h64_in = nc.dram_tensor("h64", [64, 64], F32, kind="ExternalInput").ap()
    h128_in = nc.dram_tensor("h128", [128, 128], F32, kind="ExternalInput").ap()
    mfirst = nc.dram_tensor("mfirst", [128, MASK_COLS], F32, kind="ExternalInput").ap()
    mrest = nc.dram_tensor("mrest", [128, MASK_COLS], F32, kind="ExternalInput").ap()
    out = nc.dram_tensor("out", [D, T_OWN], F32, kind="ExternalOutput").ap()

    s_w1 = nc.dram_tensor("s_w1", [HC, 128, 2048], BF16).ap()
    s_w3 = nc.dram_tensor("s_w3", [HC, 128, 2048], BF16).ap()
    s_g1t = nc.dram_tensor("s_g1t", [HC, DC, 128, 128], BF16).ap()
    s_g1p = nc.dram_tensor("s_g1p", [8, DC, 128, 1024], BF16).ap()
    s_qk = nc.dram_tensor("s_qk", [32, 128, 2048], BF16).ap()
    s_wv = nc.dram_tensor("s_wv", [4, DC, 128, 512], BF16).ap()
    s_wout = nc.dram_tensor("s_wout", [DC, 128, 2048], BF16).ap()
    z1scr = nc.dram_tensor("z1scr", [HC, 128, CH], BF16).ap()

    P = Prog()
    emit = P.emit

    with contextlib.ExitStack() as ctx:
        ent = ctx.enter_context

        xbuf = ent(nc.sbuf_tensor("xbuf", [128, DC * CH], F32))
        xln = ent(nc.sbuf_tensor("xln", [128, DC * CH], BF16))
        arena = ent(nc.sbuf_tensor("arena", [128, 11264], BF16))
        z1r = ent(nc.sbuf_tensor("z1r", [128, 8 * CH], BF16))
        silu_sb = ent(nc.sbuf_tensor("silu_sb", [128, 2 * CH], F32))
        hbf = ent(nc.sbuf_tensor("hbf", [128, 2 * CH], BF16))
        kc = [ent(nc.sbuf_tensor("kc0", [128, NH * CH], BF16)),
              ent(nc.sbuf_tensor("kc1", [128, NH * CH], BF16))]
        vc = [ent(nc.sbuf_tensor("vc0", [128, 4 * D], BF16)),
              ent(nc.sbuf_tensor("vc1", [128, 4 * D], BF16))]
        w1b = ent(nc.sbuf_tensor("w1b", [128, 3 * 1024], BF16))
        w3b = ent(nc.sbuf_tensor("w3b", [128, 3 * 1024], BF16))
        g1b = ent(nc.sbuf_tensor("g1b", [128, 6 * 1024], BF16))
        qb = ent(nc.sbuf_tensor("qb", [128, 2 * 1024], BF16))
        kb = ent(nc.sbuf_tensor("kb", [128, 2 * 1024], BF16))
        wvb = ent(nc.sbuf_tensor("wvb", [128, 3 * 512], BF16))
        woutb = ent(nc.sbuf_tensor("woutb", [128, 3 * 1024], BF16))
        maskb = ent(nc.sbuf_tensor("maskb", [128, MASK_COLS], BF16))
        xsqb = ent(nc.sbuf_tensor("xsqb", [128, 2 * CH], BF16))
        xbfb = ent(nc.sbuf_tensor("xbfb", [128, 2 * CH], BF16))
        rbsb = ent(nc.sbuf_tensor("rbsb", [128, 2 * CH], F32))
        stats = ent(nc.sbuf_tensor("stats", [128, 2 * CH], F32))
        h64b = ent(nc.sbuf_tensor("h64b", [128, 64], BF16))
        h128b = ent(nc.sbuf_tensor("h128b", [128, 128], BF16))
        onesM = ent(nc.sbuf_tensor("onesM", [128, 128], BF16))
        ebias = ent(nc.sbuf_tensor("ebias", [128, 1], F32))

        pb4 = [ent(nc.psum_tensor(f"pb{i}", [128, 512], F32)) for i in range(4)]
        pq = ent(nc.psum_tensor("pq", [128, 2048], F32))
        pb = pb4 + [pq[:, 512 * i:512 * i + 512] for i in range(4)]

        sems = {n: ent(nc.semaphore(f"s_{n}")) for n in SEM_NAMES}
        block = ent(nc.Block())

        B_x = [Buf(f"x{i}") for i in range(DC)]
        B_xln = [Buf(f"xln{i}") for i in range(DC)]
        B_obf = Buf("obf")
        B_qbf = [Buf("qbf0"), Buf("qbf1")]
        B_exp = [Buf("exp0"), Buf("exp1")]
        B_pbf = [Buf("pbf0"), Buf("pbf1")]
        B_z1 = [Buf(f"z1_{i}") for i in range(8)]
        B_silu = [Buf(f"silu{i}") for i in range(2)]
        B_hbf = [Buf(f"hbf{i}") for i in range(2)]
        B_kc = [Buf("kc0"), Buf("kc1")]
        B_vc = [Buf("vc0"), Buf("vc1")]
        B_w1 = [Buf(f"w1b{i}") for i in range(3)]
        B_w3 = [Buf(f"w3b{i}") for i in range(3)]
        B_g1 = [Buf(f"g1b{i}") for i in range(6)]
        B_qb = [Buf(f"qb{i}") for i in range(2)]
        B_kb = [Buf(f"kb{i}") for i in range(2)]
        B_wv = [Buf(f"wvb{i}") for i in range(3)]
        B_wout = [Buf(f"woutb{i}") for i in range(3)]
        B_mask = Buf("maskb")
        B_xsq = [Buf(f"xsq{i}") for i in range(2)]
        B_xbf = [Buf(f"xbf{i}") for i in range(2)]
        B_rb = [Buf(f"rb{i}") for i in range(2)]
        B_stats = [Buf(f"st{i}") for i in range(3)]
        B_pb = [Buf(f"pb{i}") for i in range(8)]
        B_h128 = Buf("h128")
        B_onesM = Buf("onesM")
        B_ebias = Buf("ebias")
        B_sw1 = [Buf(f"sw1_{i}") for i in range(16)]
        B_sw3 = [Buf(f"sw3_{i}") for i in range(16)]
        B_sg1 = Buf("sg1")
        B_sg1p = [Buf(f"sg1p_{i}") for i in range(8)]
        B_w2cf = [Buf("w2cf0"), Buf("w2cf1")]
        B_w2cb = [Buf("w2cb0"), Buf("w2cb1")]
        B_h64 = Buf("h64")
        B_sqk = [Buf(f"sqk_{i}") for i in range(8)]
        B_swv = Buf("swv")
        B_swout = [Buf(f"swout_{i}") for i in range(4)]
        B_g1st = [Buf(f"g1s{i}") for i in range(2)]

        emit("pool", lambda g: g.dma_start(out=h64b[64:128, :], in_=h64_in),
             writes=[B_h64], inc=True, dma=True)
        emit("pool", lambda g: g.dma_start(out=h64b[0:64, :], in_=h64_in),
             writes_merge=[B_h64], inc=True, dma=True)
        emit("pool", lambda g: g.dma_start(out=h128b[:, :], in_=h128_in),
             writes=[B_h128], inc=True, dma=True)
        emit("pool", lambda g: g.dma_start(out=maskb[:, :], in_=mfirst),
             writes=[B_mask], inc=True, dma=True)
        emit("dve", lambda v: v.memset(onesM[:, :], 1.0), writes=[B_onesM], inc=True)
        emit("dve", lambda v: v.memset(ebias[:, :], EXP_BIAS), writes=[B_ebias], inc=True)

        w1v = w1t.rearrange("(c p) (m f) -> m p c f", p=128, f=128)
        w3v = w3t.rearrange("(c p) (m f) -> m p c f", p=128, f=128)
        qkv_v = wqkvt[:, 0:4096].rearrange("(c p) (j f) -> j p c f", p=128, f=128)
        wv_v = wqkvt[:, 4096:6144].rearrange("(c p) (g f) -> g c p f", p=128, f=512)
        wout_v = woutt.rearrange("(c p) (j f) -> j p c f", p=128, f=128)

        def sub_cast(dst3, src3):
            return lambda g, dst3=dst3, src3=src3: g.dma_start(out=dst3, in_=src3)

        def w_sub(sten, view, m):
            return sub_cast(sten[m].rearrange("p (c f) -> p c f", f=128), view[m])

        w1_jobs = [[B_sw1[i]] + [w_sub(s_w1, w1v, 4 * i + j) for j in range(4)]
                   for i in range(16)]
        w3_jobs = [[B_sw3[i]] + [w_sub(s_w3, w3v, 4 * i + j) for j in range(4)]
                   for i in range(16)]
        tail_jobs = []
        for i in range(8):
            tail_jobs.append([B_sqk[i]] + [w_sub(s_qk, qkv_v, 4 * i + j) for j in range(4)])
        tail_jobs.append([B_swv] + [
            (lambda g, j=j: g.dma_start(out=s_wv[j:j + 1], in_=wv_v[j:j + 1]))
            for j in range(4)])
        for i in range(4):
            tail_jobs.append([B_swout[i]] + [w_sub(s_wout, wout_v, 4 * i + j)
                                             for j in range(4)])

        def emit_cast_job(job):
            b = job[0]
            for fn in job[1:]:
                emit("pool", fn, writes_merge=[b], inc=True, dma=True)

        for job in (w1_jobs[0], w3_jobs[0], w1_jobs[1], w3_jobs[1]):
            emit_cast_job(job)

        w2v = w2t.rearrange("(a b) o -> b a o", b=128)
        w2cf = [arena[64:128, 0:4096].bitcast(F32),
                arena[0:64, 0:4096].bitcast(F32)]
        w2cb = [arena[64:128, 8192:10240], arena[0:64, 8192:10240]]
        h64h = [h64b[64:128, :], h64b[0:64, :]]
        g1st = [arena[0:64, 4096:6144], arena[0:64, 6144:8192]]

        def g1_store(bidx):
            p2 = bidx % 2
            g1st_v = g1st[p2].rearrange("p (o f) -> p o f", f=128)
            emit("act",
                 lambda s, g1st_v=g1st_v, bidx=bidx: s.dma_start(
                     out=s_g1t[:, :, bidx, :], in_=g1st_v),
                 reads=[B_g1st[p2]], writes=[B_sg1], inc=True, dma=True)

        def tf_iter(bidx):
                par = bidx % 2
                p2 = bidx % 2
                emit("sp",
                     lambda s, bidx=bidx, par=par: s.dma_start(out=w2cf[par], in_=w2v[bidx]),
                     writes=[B_w2cf[par]], inc=True, dma=True)
                ceng = "act" if par == 0 else "dve"
                if ceng == "act":
                    emit("act", lambda s, par=par: s.activation(w2cb[par], w2cf[par], AF.Copy),
                         reads=[B_w2cf[par]], writes=[B_w2cb[par]], inc=True)
                else:
                    emit("dve", lambda v, par=par: v.tensor_copy(w2cb[par], w2cf[par]),
                         reads=[B_w2cf[par]], writes=[B_w2cb[par]], inc=True)
                boff = 4
                for j in range(4):
                    jp = boff + j
                    emit("pe",
                         lambda t, par=par, j=j, jp=jp: t.matmul(
                             pb[jp][0:64, :], h64h[par],
                             w2cb[par][:, 512 * j:512 * j + 512],
                             start=True, stop=True),
                         reads=[B_w2cb[par], B_h64], first_writes=[B_pb[jp]],
                         writes=[B_pb[jp]], inc=True)
                for j in range(4):
                    jb = boff + j
                    eng = "dve" if (bidx + j) % 2 == 0 else "act"
                    if eng == "act":
                        emit("act",
                             lambda s, p2=p2, j=j, jb=jb: s.activation(
                                 g1st[p2][:, 512 * j:512 * j + 512],
                                 pb[jb][0:64, :], AF.Copy, scale=G1_SCALE),
                             reads=[B_pb[jb]],
                             writes_merge=[B_g1st[p2]], inc=True)
                    else:
                        emit("dve",
                             lambda v, p2=p2, j=j, jb=jb: v.tensor_scalar(
                                 g1st[p2][:, 512 * j:512 * j + 512],
                                 pb[jb][0:64, :], G1_SCALE, None, ALU.mult),
                             reads=[B_pb[jb]],
                             writes_merge=[B_g1st[p2]], inc=True)

        flat_casts = []
        for i in range(2, 16):
            for fn in w1_jobs[i][1:]:
                flat_casts.append((fn, B_sw1[i]))
            for fn in w3_jobs[i][1:]:
                flat_casts.append((fn, B_sw3[i]))

        def emit_repack_and_fence():
            for mb in range(8):
                for o in range(DC):
                    rsrc = s_g1t[8 * mb:8 * mb + 8, o].rearrange("m p f -> p m f")
                    rdst = s_g1p[mb, o].rearrange("p (m f) -> p m f", f=128)
                    q = "act" if (mb * DC + o) % 2 else "sp"
                    emit(q, lambda s, rsrc=rsrc, rdst=rdst: s.dma_start(out=rdst, in_=rsrc),
                         reads=[B_sg1], writes_merge=[B_sg1p[mb]], inc=True, dma=True)
            for job in tail_jobs:
                emit_cast_job(job)
            emit("dve", lambda v: v.memset(arena[0:1, 0:1], 0.0),
                 writes=[B_w2cf[0], B_w2cf[1], B_w2cb[0], B_w2cb[1],
                         B_g1st[0], B_g1st[1],
                         B_obf, B_qbf[0], B_qbf[1], B_exp[0], B_exp[1],
                         B_pbf[0], B_pbf[1]],
                 inc=True)

        def mm_group(bank, items):
            n = len(items)
            for i, (reg, lh, rh, reads) in enumerate(items):
                first = i == 0
                last = i == n - 1
                emit("pe",
                     lambda t, reg=reg, lh=lh, rh=rh, first=first, last=last:
                         t.matmul(reg, lh, rh, start=first, stop=last),
                     reads=list(reads),
                     first_writes=[B_pb[bank]] if first else (),
                     writes=[B_pb[bank]] if last else (),
                     inc=last)

        def ln_tile(c):
            par = c % 2
            emit("act",
                 lambda s, c=c, par=par: s.activation(
                     xsqb[:, par * CH:(par + 1) * CH],
                     xbuf[:, c * CH:(c + 1) * CH], AF.Square),
                 reads=[B_x[c]], writes=[B_xsq[par]], inc=True)
            emit("dve",
                 lambda v, c=c, par=par: v.tensor_copy(
                     xbfb[:, par * CH:(par + 1) * CH],
                     xbuf[:, c * CH:(c + 1) * CH]),
                 reads=[B_x[c]], writes=[B_xbf[par]], inc=True)
            emit("pe",
                 lambda t, c=c, par=par: t.matmul(
                     pb[0][:, :], onesM[:, :],
                     xbfb[:, par * CH:(par + 1) * CH],
                     start=(c == 0), stop=(c == DC - 1)),
                 reads=[B_xbf[par], B_onesM],
                 first_writes=[B_pb[0]] if c == 0 else (),
                 writes=[B_pb[0]] if c == DC - 1 else (),
                 inc=True)
            emit("pe",
                 lambda t, c=c, par=par: t.matmul(
                     pb[1][:, :], onesM[:, :],
                     xsqb[:, par * CH:(par + 1) * CH],
                     start=(c == 0), stop=(c == DC - 1)),
                 reads=[B_xsq[par], B_onesM],
                 first_writes=[B_pb[1]] if c == 0 else (),
                 writes=[B_pb[1]] if c == DC - 1 else (),
                 inc=True)

        def ln_finish():
            sA = stats[:, 0:CH]
            sB = stats[:, CH:2 * CH]
            sC = rbsb[:, 0:CH]
            B_stats[2] = B_rb[0]
            emit("dve", lambda v: v.tensor_scalar(sA, pb[0][:, :], -1.0 / D, None, ALU.mult),
                 reads=[B_pb[0]], writes=[B_stats[0]], inc=True)
            emit("dve", lambda v: v.tensor_scalar(sB, pb[1][:, :], 1.0 / D, LN_EPS, ALU.mult, ALU.add),
                 reads=[B_pb[1]], writes=[B_stats[1]], inc=True)
            emit("dve", lambda v: v.tensor_mul(sC, sA, sA),
                 reads=[B_stats[0]], writes=[B_stats[2]], inc=True)
            emit("dve", lambda v: v.tensor_sub(sB, sB, sC),
                 reads=[B_stats[1], B_stats[2]], writes=[B_stats[1]], inc=True)
            emit("act", lambda s: s.activation(sC, sB, AF.Ln),
                 reads=[B_stats[1]], writes=[B_stats[2]], inc=True)
            emit("act", lambda s: s.activation(sB, sC, AF.Exp, scale=-0.5),
                 reads=[B_stats[2]], writes=[B_stats[1]], inc=True)
            for c in range(DC):
                par = c % 2
                eng = "dve"
                if eng == "dve":
                    emit("dve",
                         lambda v, c=c, par=par: v.tensor_add(
                             silu_sb[:, par * CH:(par + 1) * CH],
                             xbuf[:, c * CH:(c + 1) * CH], sA),
                         reads=[B_x[c], B_stats[0]], writes=[B_silu[par]], inc=True)
                    emit("dve",
                         lambda v, c=c, par=par: v.tensor_mul(
                             xln[:, c * CH:(c + 1) * CH],
                             silu_sb[:, par * CH:(par + 1) * CH], sB),
                         reads=[B_silu[par], B_stats[1]], writes=[B_xln[c]], inc=True)
                else:
                    emit("pool",
                         lambda g, c=c, par=par: g.tensor_add(
                             silu_sb[:, par * CH:(par + 1) * CH],
                             xbuf[:, c * CH:(c + 1) * CH], sA),
                         reads=[B_x[c], B_stats[0]], writes=[B_silu[par]], inc=True)
                    emit("pool",
                         lambda g, c=c, par=par: g.tensor_mul(
                             xln[:, c * CH:(c + 1) * CH],
                             silu_sb[:, par * CH:(par + 1) * CH], sB),
                         reads=[B_silu[par], B_stats[1]], writes=[B_xln[c]], inc=True)

        ring_state = {}

        def stream(name, rings, bufs, fn_maker, dep=None):
            idx = ring_state.get(name, 0)
            ring_state[name] = idx + 1
            slot = idx % rings
            b = bufs[slot]
            emit("sp", fn_maker(slot), reads=[dep] if dep is not None else [],
                 writes=[b], inc=True, dma=True)
            return slot, b

        def ffn(tile_cb=None):
            def emit_h128(hp, mi):
                emit("pe",
                     lambda t, hp=hp: t.matmul(
                         pb[2][:, :], h128b[:, :], hbf[:, hp * CH:(hp + 1) * CH],
                         start=True, stop=True),
                     reads=[B_hbf[hp], B_h128], first_writes=[B_pb[2]],
                     writes=[B_pb[2]], inc=True)
                emit("act",
                     lambda s, mi=mi: s.activation(
                         z1r[:, mi * CH:(mi + 1) * CH], pb[2][:, :], AF.Copy),
                     reads=[B_pb[2]], writes=[B_z1[mi]], inc=True)

            pend_h = None
            for mb in range(8):
                for mi in range(8):
                    m = mb * 8 + mi
                    halves = []
                    for half in range(2):
                        src = s_w1[m][:, 1024 * half:1024 * half + 1024]
                        s1, b1 = stream("w1", 3, B_w1, lambda slot, src=src: (
                            lambda s: s.dma_start(
                                out=w1b[:, slot * 1024:(slot + 1) * 1024], in_=src)),
                            dep=B_sw1[m // 4])
                        src3 = s_w3[m][:, 1024 * half:1024 * half + 1024]
                        s3, b3 = stream("w3", 3, B_w3, lambda slot, src3=src3: (
                            lambda s: s.dma_start(
                                out=w3b[:, slot * 1024:(slot + 1) * 1024], in_=src3)),
                            dep=B_sw3[m // 4])
                        halves.append((s1, b1, s3, b3))
                    itemsG, itemsL = [], []
                    for d in range(DC):
                        s1, b1, s3, b3 = halves[d // 8]
                        dd = d % 8
                        itemsG.append((pb[0][:, :],
                                       w1b[:, s1 * 1024 + dd * 128: s1 * 1024 + dd * 128 + 128],
                                       xln[:, d * CH:(d + 1) * CH], [b1, B_xln[d]]))
                        itemsL.append((pb[1][:, :],
                                       w3b[:, s3 * 1024 + dd * 128: s3 * 1024 + dd * 128 + 128],
                                       xln[:, d * CH:(d + 1) * CH], [b3, B_xln[d]]))
                    mm_group(0, itemsG)
                    mm_group(1, itemsL)
                    if pend_h is not None:
                        emit_h128(*pend_h)
                    sp = m % 2
                    hp = m % 2
                    emit("act",
                         lambda s, sp=sp: s.activation(
                             silu_sb[:, sp * CH:(sp + 1) * CH], pb[0][:, :], AF.Silu),
                         reads=[B_pb[0]], writes=[B_silu[sp]], inc=True)
                    emit("dve",
                         lambda v, sp=sp, hp=hp: v.tensor_mul(
                             hbf[:, hp * CH:(hp + 1) * CH],
                             silu_sb[:, sp * CH:(sp + 1) * CH], pb[1][:, :]),
                         reads=[B_silu[sp], B_pb[1]], writes=[B_hbf[hp]], inc=True)
                    pend_h = (hp, mi)
                if pend_h is not None:
                    emit_h128(*pend_h)
                    pend_h = None
                for g in range(4):
                    slabs = []
                    for oi in range(4):
                        o = 4 * g + oi
                        src = s_g1p[mb, o]
                        sg, bg = stream("g1", 6, B_g1, lambda slot, src=src: (
                            lambda s: s.dma_start(
                                out=g1b[:, slot * 1024:(slot + 1) * 1024], in_=src)),
                            dep=B_sg1p[mb])
                        slabs.append((sg, bg))
                    for oi in range(4):
                        o = 4 * g + oi
                        sg, bg = slabs[oi]
                        bank = 4 + oi
                        items = []
                        for mi in range(8):
                            off = sg * 1024 + mi * 128
                            items.append((pb[bank][:, :], g1b[:, off:off + 128],
                                          z1r[:, mi * CH:(mi + 1) * CH],
                                          [bg, B_z1[mi]]))
                        mm_group(bank, items)
                        emit("dve",
                             lambda v, o=o, bank=bank: v.tensor_add(
                                 xbuf[:, o * CH:(o + 1) * CH],
                                 xbuf[:, o * CH:(o + 1) * CH], pb[bank][:, :]),
                             reads=[B_x[o], B_pb[bank]], writes=[B_x[o]], inc=True)
                        if tile_cb is not None and mb == 7:
                            tile_cb(o)

        def make_prod_spill():
            state = {"pend": None}

            def emit_h128_spill(hp, m):
                mi8 = m % 8
                emit("pe",
                     lambda t, hp=hp: t.matmul(
                         pb[2][:, :], h128b[:, :], hbf[:, hp * CH:(hp + 1) * CH],
                         start=True, stop=True),
                     reads=[B_hbf[hp], B_h128], first_writes=[B_pb[2]],
                     writes=[B_pb[2]], inc=True)
                emit("act",
                     lambda s, mi8=mi8: s.activation(
                         z1r[:, mi8 * CH:(mi8 + 1) * CH], pb[2][:, :], AF.Copy),
                     reads=[B_pb[2]], writes=[B_z1[mi8]], inc=True)
                emit("act",
                     lambda s, m=m, mi8=mi8: s.dma_start(
                         out=z1scr[m], in_=z1r[:, mi8 * CH:(mi8 + 1) * CH]),
                     reads=[B_z1[mi8]], inc=True, dma=True)

            def one_m(m):
                halves = []
                for half in range(2):
                    src = s_w1[m][:, 1024 * half:1024 * half + 1024]
                    s1, b1 = stream("w1", 3, B_w1, lambda slot, src=src: (
                        lambda s: s.dma_start(
                            out=w1b[:, slot * 1024:(slot + 1) * 1024], in_=src)),
                        dep=B_sw1[m // 4])
                    src3 = s_w3[m][:, 1024 * half:1024 * half + 1024]
                    s3, b3 = stream("w3", 3, B_w3, lambda slot, src3=src3: (
                        lambda s: s.dma_start(
                            out=w3b[:, slot * 1024:(slot + 1) * 1024], in_=src3)),
                        dep=B_sw3[m // 4])
                    halves.append((s1, b1, s3, b3))
                itemsG, itemsL = [], []
                for d in range(DC):
                    s1, b1, s3, b3 = halves[d // 8]
                    dd = d % 8
                    itemsG.append((pb[0][:, :],
                                   w1b[:, s1 * 1024 + dd * 128: s1 * 1024 + dd * 128 + 128],
                                   xln[:, d * CH:(d + 1) * CH], [b1, B_xln[d]]))
                    itemsL.append((pb[1][:, :],
                                   w3b[:, s3 * 1024 + dd * 128: s3 * 1024 + dd * 128 + 128],
                                   xln[:, d * CH:(d + 1) * CH], [b3, B_xln[d]]))
                mm_group(0, itemsG)
                mm_group(1, itemsL)
                if state["pend"] is not None:
                    emit_h128_spill(*state["pend"])
                sp = m % 2
                hp = m % 2
                emit("act",
                     lambda s, sp=sp: s.activation(
                         silu_sb[:, sp * CH:(sp + 1) * CH], pb[0][:, :], AF.Silu),
                     reads=[B_pb[0]], writes=[B_silu[sp]], inc=True)
                emit("dve",
                     lambda v, sp=sp, hp=hp: v.tensor_mul(
                         hbf[:, hp * CH:(hp + 1) * CH],
                         silu_sb[:, sp * CH:(sp + 1) * CH], pb[1][:, :]),
                     reads=[B_silu[sp], B_pb[1]], writes=[B_hbf[hp]], inc=True)
                state["pend"] = (hp, m)

            def finish():
                if state["pend"] is not None:
                    emit_h128_spill(*state["pend"])
                    state["pend"] = None

            return one_m, finish

        def ffn_g1_pass(tile_cb=None):
            for mb in range(8):
                for mi in range(8):
                    m = 8 * mb + mi
                    emit("sp",
                         lambda s, m=m, mi=mi: s.dma_start(
                             out=z1r[:, mi * CH:(mi + 1) * CH], in_=z1scr[m]),
                         writes=[B_z1[mi]], inc=True, dma=True)
                for g in range(4):
                    slabs = []
                    for oi in range(4):
                        o = 4 * g + oi
                        src = s_g1p[mb, o]
                        sg, bg = stream("g1", 6, B_g1, lambda slot, src=src: (
                            lambda s: s.dma_start(
                                out=g1b[:, slot * 1024:(slot + 1) * 1024], in_=src)),
                            dep=B_sg1p[mb])
                        slabs.append((sg, bg))
                    for oi in range(4):
                        o = 4 * g + oi
                        sg, bg = slabs[oi]
                        bank = 4 + oi
                        items = []
                        for mi in range(8):
                            off = sg * 1024 + mi * 128
                            items.append((pb[bank][:, :], g1b[:, off:off + 128],
                                          z1r[:, mi * CH:(mi + 1) * CH],
                                          [bg, B_z1[mi]]))
                        mm_group(bank, items)
                        emit("dve",
                             lambda v, o=o, bank=bank: v.tensor_add(
                                 xbuf[:, o * CH:(o + 1) * CH],
                                 xbuf[:, o * CH:(o + 1) * CH], pb[bank][:, :]),
                             reads=[B_x[o], B_pb[bank]], writes=[B_x[o]], inc=True)
                        if tile_cb is not None and mb == 7:
                            tile_cb(o)

        def attention(ci, tile_cb=None):
            cur = ci % 2
            prev = 1 - cur
            kcur, kprev = kc[cur], kc[prev]
            vcur, vprev = vc[cur], vc[prev]
            Bkc, Bkp = B_kc[cur], B_kc[prev]
            Bvc, Bvp = B_vc[cur], B_vc[prev]

            for g in range(4):
                for d in range(DC):
                    src = s_wv[g, d]
                    sv, bv = stream("wv", 3, B_wv, lambda slot, src=src: (
                        lambda s: s.dma_start(
                            out=wvb[:, slot * 512:(slot + 1) * 512], in_=src)),
                        dep=B_swv)
                    for ts in range(4):
                        bank = 4 + ts
                        first = d == 0
                        last = d == DC - 1
                        emit("pe",
                             lambda t, sv=sv, d=d, ts=ts, bank=bank, first=first, last=last:
                                 t.matmul(pb[bank][:, :],
                                          xln[:, d * CH + ts * 128: d * CH + ts * 128 + 128],
                                          wvb[:, sv * 512:(sv + 1) * 512],
                                          start=first, stop=last),
                             reads=[bv, B_xln[d]],
                             first_writes=[B_pb[bank]] if first else (),
                             writes=[B_pb[bank]] if last else (),
                             inc=last)
                for ts in range(4):
                    bank = 4 + ts
                    emit("act",
                         lambda s, g=g, ts=ts, bank=bank, vcur=vcur: s.activation(
                             vcur[:, ts * D + g * 512: ts * D + g * 512 + 512],
                             pb[bank][:, :], AF.Copy),
                         reads=[B_pb[bank]], writes=[Bvc], inc=True)

            o_bf = arena[:, 0:NH * CH]
            qs_ = [arena[:, 8192:8704], arena[:, 8704:9216]]
            expb = [arena[:, 9216:9728], arena[:, 9728:10240]]
            p_s = [arena[:, 10240:10752], arena[:, 10752:11264]]

            for h in range(NH):
                qh, kh = [], []
                for half in range(2):
                    srcq = s_qk[h][:, 1024 * half:1024 * half + 1024]
                    sq, bq = stream("qk_q", 2, B_qb, lambda slot, srcq=srcq: (
                        lambda s: s.dma_start(
                            out=qb[:, slot * 1024:(slot + 1) * 1024], in_=srcq)),
                        dep=B_sqk[h // 4])
                    qh.append((sq, bq))
                    srck = s_qk[16 + h][:, 1024 * half:1024 * half + 1024]
                    sk, bk = stream("qk_k", 2, B_kb, lambda slot, srck=srck: (
                        lambda s: s.dma_start(
                            out=kb[:, slot * 1024:(slot + 1) * 1024], in_=srck)),
                        dep=B_sqk[4 + h // 4])
                    kh.append((sk, bk))
                items = []
                for d in range(DC):
                    sk, bk = kh[d // 8]
                    dd = d % 8
                    items.append((pb[1][:, :],
                                  kb[:, sk * 1024 + dd * 128: sk * 1024 + dd * 128 + 128],
                                  xln[:, d * CH:(d + 1) * CH], [bk, B_xln[d]]))
                mm_group(1, items)
                emit("act",
                     lambda s, h=h, kcur=kcur: s.activation(
                         kcur[:, h * CH:(h + 1) * CH], pb[1][:, :], AF.Copy),
                     reads=[B_pb[1]], writes=[Bkc], inc=True)
                if ci == 0:
                    continue
                items = []
                for d in range(DC):
                    sq, bq = qh[d // 8]
                    dd = d % 8
                    items.append((pb[0][:, :],
                                  qb[:, sq * 1024 + dd * 128: sq * 1024 + dd * 128 + 128],
                                  xln[:, d * CH:(d + 1) * CH], [bq, B_xln[d]]))
                mm_group(0, items)
                q_bf = qs_[h % 2]
                B_q = B_qbf[h % 2]
                emit("act",
                     lambda s, q_bf=q_bf: s.activation(q_bf, pb[0][:, :], AF.Copy),
                     reads=[B_pb[0]], writes=[B_q], inc=True)

                for ti, t8 in enumerate((3, 4, 0, 1, 2, 5, 6, 7)):
                    lo, hi, w = ST_LO[t8], ST_HI[t8], ST_W[t8]
                    moff = ST_OFF[t8]
                    stp = 2 + (ti % 2)
                    ksrc = kprev if t8 < 4 else kcur
                    Bks = Bkp if t8 < 4 else Bkc
                    koff = h * CH + (t8 % 4) * 128
                    emit("pe",
                         lambda t, stp=stp, ksrc=ksrc, koff=koff, lo=lo, hi=hi, q_bf=q_bf:
                             t.matmul(pb[stp][:, 0:hi - lo],
                                      ksrc[:, koff:koff + 128],
                                      q_bf[:, lo:hi], start=True, stop=True),
                         reads=[Bks, B_q], first_writes=[B_pb[stp]],
                         writes=[B_pb[stp]], inc=True)
                    ep = t8 % 2
                    pp = ti % 2
                    p0 = p_s[pp]
                    emit("act",
                         lambda s, stp=stp, ep=ep, w=w: s.activation(
                             expb[ep][:, 0:w], pb[stp][:, 0:w], AF.Exp,
                             bias=ebias[:, 0:1], scale=SCALE),
                         reads=[B_pb[stp], B_ebias], writes=[B_exp[ep]], inc=True)
                    emit("dve",
                         lambda v, ep=ep, w=w, moff=moff, p0=p0: v.tensor_mul(
                             p0[:, 0:w], expb[ep][:, 0:w], maskb[:, moff:moff + w]),
                         reads=[B_exp[ep], B_mask], writes=[B_pbf[pp]], inc=True)
                    emit("pe",
                         lambda t, lo=lo, w=w, ti=ti, p0=p0: t.matmul(
                             pb[4][:, lo:lo + w], onesM[:, :], p0[:, 0:w],
                             start=(ti == 0), stop=(ti == 7)),
                         reads=[B_pbf[pp], B_onesM],
                         first_writes=[B_pb[4]] if ti == 0 else (),
                         writes=[B_pb[4]] if ti == 7 else (),
                         inc=True)
                    vsrc = vprev if t8 < 4 else vcur
                    Bvs = Bvp if t8 < 4 else Bvc
                    voff = (t8 % 4) * D + h * 128
                    emit("pe",
                         lambda t, vsrc=vsrc, voff=voff, lo=lo, w=w, ti=ti, p0=p0: t.matmul(
                             pb[5][:, lo:lo + w], vsrc[:, voff:voff + 128], p0[:, 0:w],
                             start=(ti == 0), stop=(ti == 7)),
                         reads=[B_pbf[pp], Bvs],
                         first_writes=[B_pb[5]] if ti == 0 else (),
                         writes=[B_pb[5]] if ti == 7 else (),
                         inc=True)
                rp = h % 2
                emit("dve",
                     lambda v, rp=rp: v.tensor_copy(rbsb[:, rp * CH:(rp + 1) * CH],
                                                    pb[4][:, :]),
                     reads=[B_pb[4]], writes=[B_rb[rp]], inc=True)
                emit("dve",
                     lambda v, rp=rp: v.reciprocal(rbsb[:, rp * CH:(rp + 1) * CH],
                                                   rbsb[:, rp * CH:(rp + 1) * CH]),
                     reads=[B_rb[rp]], writes=[B_rb[rp]], inc=True)
                emit("dve",
                     lambda v, h=h, rp=rp: v.tensor_mul(
                         o_bf[:, h * CH:(h + 1) * CH],
                         rbsb[:, rp * CH:(rp + 1) * CH], pb[5][:, :]),
                     reads=[B_rb[rp], B_pb[5]], writes=[B_obf], inc=True)

            if ci == 0:
                return
            for o in range(DC):
                wh = []
                for half in range(2):
                    src = s_wout[o][:, 1024 * half:1024 * half + 1024]
                    so, bo = stream("wout", 3, B_wout, lambda slot, src=src: (
                        lambda s: s.dma_start(
                            out=woutb[:, slot * 1024:(slot + 1) * 1024], in_=src)),
                        dep=B_swout[o // 4])
                    wh.append((so, bo))
                bank = 2 + o % 2
                items = []
                for d in range(DC):
                    so, bo = wh[d // 8]
                    dd = d % 8
                    items.append((pb[bank][:, :],
                                  woutb[:, so * 1024 + dd * 128: so * 1024 + dd * 128 + 128],
                                  o_bf[:, d * CH:(d + 1) * CH], [bo, B_obf]))
                mm_group(bank, items)
                emit("dve",
                     lambda v, o=o, bank=bank: v.tensor_add(
                         xbuf[:, o * CH:(o + 1) * CH],
                         xbuf[:, o * CH:(o + 1) * CH], pb[bank][:, :]),
                     reads=[B_x[o], B_pb[bank]], writes=[B_x[o]], inc=True)
                if tile_cb is not None:
                    tile_cb(o)

        def load_x(ci):
            for c in range(DC):
                emit("sp",
                     lambda s, ci=ci, c=c: s.dma_start(
                         out=xbuf[:, c * CH:(c + 1) * CH],
                         in_=xin[c * 128:(c + 1) * 128, ci * CH:(ci + 1) * CH]),
                     writes=[B_x[c]], inc=True, dma=True)
                ln_tile(c)
            ln_finish()

        load_x(0)
        prod_m, prod_finish = make_prod_spill()
        ci0 = 0
        fc_i = 0
        for bidx in range(128):
            if fc_i < len(flat_casts):
                fn, b = flat_casts[fc_i]
                fc_i += 1
                emit("pool", fn, writes_merge=[b], inc=True, dma=True)
            tf_iter(bidx)
            if bidx >= 1:
                g1_store(bidx - 1)
            if bidx % 2 == 1:
                prod_m(bidx // 2)
        g1_store(127)
        prod_finish()
        while fc_i < len(flat_casts):
            fn, b = flat_casts[fc_i]
            fc_i += 1
            emit("pool", fn, writes_merge=[b], inc=True, dma=True)
        emit_repack_and_fence()
        ffn_g1_pass(tile_cb=ln_tile)
        ln_finish()
        attention(0)

        for ci in range(1, NCH_OWN + 1):
            load_x(ci)
            ffn(tile_cb=ln_tile)
            ln_finish()
            attention(ci, tile_cb=ln_tile)
            ln_finish()
            ffn()
            for c in range(DC):
                emit("act",
                     lambda s, ci=ci, c=c: s.dma_start(
                         out=out[c * 128:(c + 1) * 128, (ci - 1) * CH:ci * CH],
                         in_=xbuf[:, c * CH:(c + 1) * CH]),
                     reads=[B_x[c]], inc=True, dma=True)
            if ci == 1:
                emit("pool", lambda g: g.dma_start(out=maskb[:, :], in_=mrest),
                     writes=[B_mask], inc=True, dma=True)

        qa_finals = {k: P.cnt[k] for k in SEM_NAMES if k.startswith("qa") and P.cnt[k] > 0}

        def runner(phys, handle):
            for waits, fn, inc_key, amt in P.ops[phys]:
                for k, v in waits.items():
                    handle.wait_ge(sems[k], v)
                if fn is None:
                    if inc_key is not None:
                        handle.nop().then_inc(sems[inc_key], amt)
                    continue
                inst = fn(handle)
                if inc_key is not None:
                    inst.then_inc(sems[inc_key], amt)

        @block.tensor
        def _(t):
            runner("pe", t)

        @block.scalar
        def _(s):
            runner("act", s)
            for k, v in qa_finals.items():
                s.wait_ge(sems[k], v)

        @block.vector
        def _(v):
            runner("dve", v)

        @block.gpsimd
        def _(g):
            runner("pool", g)

        @block.sync
        def _(s):
            runner("sp", s)

    return nc


def kernel(L0_attn_x, L0_attn_w_qkv, L0_attn_w_out, L0_ffn_W1, L0_ffn_W2, L0_ffn_W3):
    x = np.asarray(L0_attn_x, dtype=np.float32)[0]
    w_qkv = np.asarray(L0_attn_w_qkv, dtype=np.float32)
    w_out = np.asarray(L0_attn_w_out, dtype=np.float32)
    W1 = np.asarray(L0_ffn_W1, dtype=np.float32)
    W2 = np.asarray(L0_ffn_W2, dtype=np.float32)
    W3 = np.asarray(L0_ffn_W3, dtype=np.float32)

    w1t = np.ascontiguousarray(W1.T)
    w3t = np.ascontiguousarray(W3.T)
    w2t = np.ascontiguousarray(W2.T)
    wqkvt = np.ascontiguousarray(w_qkv.T)
    woutt = np.ascontiguousarray(w_out.T)
    h64 = _hadamard(64)
    h128 = _hadamard(128)
    mrest = _band_mask(False)
    mfirst0 = _band_mask(True)

    nc = build_nc()
    in_maps = []
    for c in range(NCORES):
        r0 = 2048 * c
        if c == 0:
            halo = np.zeros((CH, D), dtype=np.float32)
        else:
            halo = x[r0 - CH:r0]
        xc = np.concatenate([halo, x[r0:r0 + T_OWN]], axis=0)
        xin = np.ascontiguousarray(xc.T)
        in_maps.append({
            "xin": xin,
            "w1t": w1t, "w3t": w3t, "w2t": w2t,
            "wqkvt": wqkvt, "woutt": woutt,
            "h64": h64, "h128": h128,
            "mfirst": mfirst0 if c == 0 else mrest,
            "mrest": mrest,
        })

    res = run_bass_kernel_spmd(nc, in_maps, core_ids=list(range(NCORES)))
    outs = []
    for c in range(NCORES):
        o = res.results[c]["out"]
        outs.append(np.ascontiguousarray(o.T))
    full = np.concatenate(outs, axis=0)
    return full.reshape(1, 16384, 2048).astype(np.float32)
